# version 02
# baseline (speedup 1.0000x reference)
import numpy as np
import ml_dtypes
from contextlib import ExitStack

import concourse.bass as bass
import concourse.mybir as mybir
from concourse.bass_utils import run_bass_kernel_spmd
from concourse.alu_op_type import AluOpType as ALU

F32 = mybir.dt.float32
BF16 = mybir.dt.bfloat16
AF = mybir.ActivationFunctionType
AX = mybir.AxisListType
BF16NP = ml_dtypes.bfloat16

NCORES = 8
T_CORE = 2048
D = 1024
EPS = 1e-5


ENGINES = ("pe", "act", "dve", "pool", "sp")
DMA_QUEUES = ("qsp", "qact", "qpool", "qdve")
QUEUE_ENGINE = {"qsp": "sp", "qact": "act", "qpool": "pool", "qdve": "dve"}


DMA_SLOTS = 8


class Op:
    def __init__(self, engine, fn, reads, writes, is_dma=False, queue=None):
        self.engine = engine
        self.fn = fn
        self.reads = reads
        self.writes = writes
        self.is_dma = is_dma
        self.queue = queue
        self.inc_needed = is_dma
        self.inc_val = 16 if is_dma else 1
        self.slot = None
        self.gen = None


class Prog:

    def __init__(self, nc):
        self.nc = nc
        self.stack = ExitStack()
        self.ops = []
        self._track = {}
        self._sb_bytes = 0
        self._dma_counts = {}
        self._eng_counts = {}

    def sbuf(self, name, shape, dtype):
        h = self.stack.enter_context(self.nc.sbuf_tensor(name, list(shape), dtype))
        bpe = 2 if dtype == BF16 else 4
        per_part = int(np.prod(shape[1:])) * bpe
        self._sb_bytes += per_part
        return h

    def psum(self, name, shape, dtype=F32):
        return self.stack.enter_context(self.nc.psum_tensor(name, list(shape), dtype))

    def sbuf_usage_kb(self):
        return self._sb_bytes / 1024.0

    def _st(self, h):
        k = id(h)
        if k not in self._track:
            self._track[k] = {"writers": [], "readers": []}
        return self._track[k]

    def _key(self, op):
        return op.queue if op.is_dma else op.engine

    SAME_ENGINE_SAFE_DIST = 4

    def record(self, engine, fn, reads=(), writes=(), is_dma=False, queue=None):
        op = Op(engine, fn, list(reads), list(writes), is_dma, queue)
        op.seq_tmp = len(self.ops)
        if is_dma:
            n = self._dma_counts.get(queue, 0)
            self._dma_counts[queue] = n + 1
            op.slot = n % DMA_SLOTS
            op.gen = n // DMA_SLOTS
        me = None if is_dma else engine
        eidx = self._eng_counts.get(engine, 0)
        self._eng_counts[engine] = eidx + 1
        op.eidx = eidx
        waits = {}

        def need(dep_op):
            if dep_op.is_dma:
                dk = ("dslot", dep_op.queue, dep_op.slot)
            else:
                dk = self._key(dep_op)
                if dk == me:
                    if op.eidx - dep_op.eidx > self.SAME_ENGINE_SAFE_DIST:
                        return
            dep_op.inc_needed = True
            prev = waits.get(dk)
            if prev is None or dep_op.seq_tmp > prev.seq_tmp:
                waits[dk] = dep_op

        for h in op.reads:
            st = self._st(h)
            for w in st["writers"]:
                need(w)
        for h in op.writes:
            st = self._st(h)
            for w in st["writers"]:
                need(w)
            for r in st["readers"]:
                need(r)
        op.wait_ops = list(waits.values())
        for h in op.reads:
            self._st(h)["readers"].append(op)
        for h in op.writes:
            st = self._st(h)
            st["writers"] = [op]
            st["readers"] = []
        self.ops.append(op)
        return op

    def dma(self, dst, src, queue="qsp", reads=(), writes=(), accum=False):
        dh, dap = dst if isinstance(dst, tuple) else (dst, dst[:])
        sh, sap = src if isinstance(src, tuple) else (src, src[:])
        eng = QUEUE_ENGINE[queue]
        if accum:
            assert queue == "qpool", "accum DMA is SWDGE (gpsimd) only"
            fn = lambda e: e.dma_start(dap, sap, accum_op=ALU.add)
            rd = [sh, dh] + list(reads)
        else:
            fn = lambda e: e.dma_start(dap, sap)
            rd = [sh] + list(reads)
        return self.record(eng, fn, reads=rd, writes=[dh] + list(writes),
                           is_dma=True, queue=queue)

    def constcol(self, val):
        if not hasattr(self, "_consts"):
            self._consts = {}
        if val not in self._consts:
            h = self.sbuf(f"const{len(self._consts)}", [128, 1], F32)
            self.dve(lambda e, h=h, val=val: e.memset(h[:], float(val)),
                     reads=[], writes=[h])
            self._consts[val] = h
        return self._consts[val]

    def pe(self, fn, reads, writes):
        return self.record("pe", fn, reads, writes)

    def act(self, fn, reads, writes):
        return self.record("act", fn, reads, writes)

    def dve(self, fn, reads, writes):
        return self.record("dve", fn, reads, writes)

    def pool(self, fn, reads, writes):
        return self.record("pool", fn, reads, writes)

    def emit(self):
        counts = {}
        for op in self.ops:
            if op.is_dma:
                op.inc_total = None
                continue
            k = self._key(op)
            if op.inc_needed:
                counts[k] = counts.get(k, 0) + op.inc_val
                op.inc_total = counts[k]
            else:
                op.inc_total = None

        sems = {}
        for k in sorted(counts):
            sems[k] = self.stack.enter_context(self.nc.semaphore(f"sem_{k}"))
        for q, n in sorted(self._dma_counts.items()):
            for sl in range(min(n, DMA_SLOTS)):
                sems[("dslot", q, sl)] = self.stack.enter_context(
                    self.nc.semaphore(f"sem_{q}_{sl}"))

        per_engine = {e: [] for e in ENGINES}
        for op in self.ops:
            wl = []
            for dep in op.wait_ops:
                if dep.is_dma:
                    wl.append((("dslot", dep.queue, dep.slot), 16 * (dep.gen + 1)))
                else:
                    if dep.inc_total is None:
                        raise RuntimeError("dep without inc")
                    wl.append((self._key(dep), dep.inc_total))
            op.resolved_waits = wl
            per_engine[op.engine].append(op)

        block = self.stack.enter_context(self.nc.Block())
        highwater = {e: {} for e in ENGINES}

        def emit_engine(eng_handle, eng_name):
            hw = highwater[eng_name]
            for op in per_engine[eng_name]:
                if op.is_dma and op.gen > 0:
                    sk = ("dslot", op.queue, op.slot)
                    val = 16 * op.gen
                    if hw.get(sk, 0) < val:
                        eng_handle.wait_ge(sems[sk], val)
                        hw[sk] = val
                for sk, val in op.resolved_waits:
                    if hw.get(sk, 0) >= val:
                        continue
                    eng_handle.wait_ge(sems[sk], val)
                    hw[sk] = val
                inst = op.fn(eng_handle)
                if op.is_dma:
                    inst.then_inc(sems[("dslot", op.queue, op.slot)], 16)
                elif op.inc_needed:
                    inst.then_inc(sems[self._key(op)], op.inc_val)

        if per_engine["pe"]:
            @block.tensor
            def _(t):
                emit_engine(t, "pe")
        if per_engine["act"]:
            @block.scalar
            def _(s):
                emit_engine(s, "act")
        if per_engine["dve"]:
            @block.vector
            def _(v):
                emit_engine(v, "dve")
        if per_engine["pool"]:
            @block.gpsimd
            def _(g):
                emit_engine(g, "pool")
        if per_engine["sp"]:
            @block.sync
            def _(sp):
                emit_engine(sp, "sp")

    def finish(self):
        op = self.record("sp", lambda e: e.nop(), reads=[], writes=[])
        waits = {}
        for dep in self.ops[:-1]:
            if dep.is_dma:
                k = ("dslot", dep.queue, dep.slot)
            else:
                k = self._key(dep)
                if k == "sp":
                    continue
            if k not in waits or dep.seq_tmp > waits[k].seq_tmp:
                waits[k] = dep
        for dep in waits.values():
            dep.inc_needed = True
        op.wait_ops = list(waits.values())
        self.emit()



def cdiv(a, b):
    return (a + b - 1) // b


def to_bf16(x):
    return np.ascontiguousarray(x).astype(BF16NP)


def f32c(x):
    return np.ascontiguousarray(x, dtype=np.float32)


def colsplit(vec, p=128):
    v = np.asarray(vec, dtype=np.float32)
    n = v.shape[0]
    assert n % p == 0
    return np.ascontiguousarray(v.reshape(n // p, p).T)



def build_L1():
    nc = bass.Bass("TRN2", target_bir_lowering=False)
    TH = T_CORE + 3
    D_INPROJ = 4256
    D_CONVDIM = 2176

    x_in = nc.dram_tensor("x", [TH, D], F32, kind="ExternalInput").ap()
    mask3 = nc.dram_tensor("mask3", [128, 3], F32, kind="ExternalInput").ap()
    wipT = nc.dram_tensor("wipT", [D, D_INPROJ], F32, kind="ExternalInput").ap()
    lnwb = nc.dram_tensor("lnwb", [128, 16], F32, kind="ExternalInput").ap()
    convw = nc.dram_tensor("convw", [128, 17 * 4], F32, kind="ExternalInput").ap()
    convb = nc.dram_tensor("convb", [128, 17], F32, kind="ExternalInput").ap()
    dtbias = nc.dram_tensor("dtbias", [32, 1], F32, kind="ExternalInput").ap()
    alog = nc.dram_tensor("alog", [32, 1], F32, kind="ExternalInput").ap()
    wq = nc.dram_tensor("wq", [D, D], F32, kind="ExternalInput").ap()
    wk = nc.dram_tensor("wk", [D, D], F32, kind="ExternalInput").ap()
    wv = nc.dram_tensor("wv", [D, D], F32, kind="ExternalInput").ap()
    bqkv = nc.dram_tensor("bqkv", [128, 24], F32, kind="ExternalInput").ap()

    xn_out = nc.dram_tensor("xn", [T_CORE, D], F32, kind="ExternalOutput").ap()
    zT_out = nc.dram_tensor("zT", [2048, TH], F32, kind="ExternalOutput").ap()
    xbcT_out = nc.dram_tensor("xbcT", [D_CONVDIM, T_CORE], F32, kind="ExternalOutput").ap()
    dt_out = nc.dram_tensor("dt", [32, TH], F32, kind="ExternalOutput").ap()
    da_out = nc.dram_tensor("da", [32, TH], F32, kind="ExternalOutput").ap()
    qT_out = nc.dram_tensor("qT", [D, T_CORE], BF16, kind="ExternalOutput").ap()
    kT_out = nc.dram_tensor("kT", [D, T_CORE], BF16, kind="ExternalOutput").ap()
    vT_out = nc.dram_tensor("vT", [D, T_CORE], BF16, kind="ExternalOutput").ap()

    p = Prog(nc)
    s_mask3 = p.sbuf("s_mask3", [128, 3], F32)
    s_lnwb = p.sbuf("s_lnwb", [128, 16], F32)
    s_convw = []
    s_convb = p.sbuf("s_convb", [128, 17], F32)
    s_dtb = p.sbuf("s_dtb", [32, 1], F32)
    s_alog = p.sbuf("s_alog", [32, 1], F32)
    s_bqkv = p.sbuf("s_bqkv", [128, 24], F32)

    p.dma(s_mask3, (mask3, mask3), queue="qsp")
    p.dma(s_lnwb, (lnwb, lnwb), queue="qsp")
    p.dma(s_dtb, (dtbias, dtbias), queue="qsp")
    p.dma(s_alog, (alog, alog), queue="qsp")
    p.dma(s_bqkv, (bqkv, bqkv), queue="qsp")
    s_convw = p.sbuf("s_convw", [128, 17 * 4], F32)
    p.dma(s_convw, (convw, convw), queue="qsp")
    p.dma(s_convb, (convb, convb), queue="qsp")

    wstage = p.sbuf("wstage", [128, D_INPROJ // 2], F32)
    wbfT = [p.sbuf(f"wbfT{c}", [128, D_INPROJ], BF16) for c in range(8)]
    for c in range(8):
        for h in range(2):
            h0 = h * (D_INPROJ // 2)
            p.dma(wstage, (wipT, wipT[128 * c:128 * (c + 1), h0:h0 + D_INPROJ // 2]),
                  queue="qpool")
            p.act(lambda e, c=c, h0=h0: e.activation(
                wbfT[c][:, h0:h0 + D_INPROJ // 2], wstage[:], AF.Copy),
                reads=[wstage], writes=[wbfT[c]])

    QKV_OFF = {"q": 0, "k": 1024, "v": 2048}

    def load_qkv_weights():
        for (w_ap, off) in ((wq, 0), (wk, 1024), (wv, 2048)):
            for c in range(8):
                p.dma((wstage, wstage[:, 0:D]), (w_ap, w_ap[128 * c:128 * (c + 1), :]),
                      queue="qpool")
                p.act(lambda e, c=c, off=off: e.activation(
                    wbfT[c][:, off:off + D], wstage[:, 0:D], AF.Copy),
                    reads=[wstage], writes=[wbfT[c]])

    tiles = [(0, 3)] + [(3 + 128 * i, 128) for i in range(16)]

    xtile = p.sbuf("xtile", [128, D], F32)
    stats = p.sbuf("stats", [128, 4], F32)
    xn_f32 = p.sbuf("xn_f32", [128, D], F32)
    xn_bf = p.sbuf("xn_bf", [128, D], BF16)
    hn_bf = p.sbuf("hn_bf", [128, D], BF16)
    ptr_ps = p.psum("ptr_ps", [128, D], BF16)
    xnT = [p.sbuf(f"xnT{c}", [128, TH], BF16) for c in range(8)]
    hnT = [p.sbuf(f"hnT{c}", [128, TH], BF16) for c in range(8)]
    ident_bf = p.sbuf("ident_bf", [128, 128], BF16)
    iden_in = nc.dram_tensor("ident", [128, 128], BF16, kind="ExternalInput").ap()
    p.dma(ident_bf, (iden_in, iden_in), queue="qsp")

    for (t0, tn) in tiles:
        p.dma((xtile, xtile[0:tn, :]), (x_in, x_in[t0:t0 + tn, :]), queue="qsp")
        p.dve(lambda e, tn=tn: e.reduce_sum(stats[0:tn, 0:1], xtile[0:tn, :], axis=AX.X),
              reads=[xtile], writes=[stats])
        p.dve(lambda e, tn=tn: e.tensor_scalar(stats[0:tn, 0:1], stats[0:tn, 0:1],
                                               1.0 / D, -1.0, op0=ALU.mult, op1=ALU.mult),
              reads=[stats], writes=[stats])
        sq = xn_f32
        p.act(lambda e, tn=tn: e.activation(sq[0:tn, :], xtile[0:tn, :], AF.Square,
                                            bias=stats[0:tn, 0:1],
                                            accum_out=stats[0:tn, 1:2]),
              reads=[xtile, stats], writes=[xn_f32, stats])
        eps_c = p.constcol(EPS)
        p.act(lambda e, tn=tn: e.activation(stats[0:tn, 2:3], stats[0:tn, 1:2], AF.Ln,
                                            scale=1.0 / D, bias=eps_c[0:tn, 0:1]),
              reads=[stats, eps_c], writes=[stats])
        p.act(lambda e, tn=tn: e.activation(stats[0:tn, 2:3], stats[0:tn, 2:3], AF.Exp,
                                            scale=-0.5),
              reads=[stats], writes=[stats])
        p.dve(lambda e, tn=tn: e.tensor_tensor(stats[0:tn, 3:4], stats[0:tn, 0:1],
                                               stats[0:tn, 2:3], op=ALU.mult),
              reads=[stats], writes=[stats])
        p.act(lambda e, tn=tn: e.activation(xn_f32[0:tn, :], xtile[0:tn, :], AF.Identity,
                                            scale=stats[0:tn, 2:3],
                                            bias=stats[0:tn, 3:4]),
              reads=[xtile, stats], writes=[xn_f32])
        if tn == 128:
            p.dma((xn_out, xn_out[t0 - 3:t0 - 3 + tn, :]), (xn_f32, xn_f32[0:tn, :]),
                  queue="qact")
        p.act(lambda e, tn=tn: e.activation(xn_bf[0:tn, :], xn_f32[0:tn, :], AF.Copy),
              reads=[xn_f32], writes=[xn_bf])
        for c in range(8):
            p.pe(lambda e, c=c, tn=tn: e.transpose(
                ptr_ps[:, 128 * c:128 * c + tn], xn_bf[0:tn, 128 * c:128 * (c + 1)],
                ident_bf[0:tn, 0:tn]),
                reads=[xn_bf, ident_bf], writes=[ptr_ps])
        for c in range(8):
            p.dve(lambda e, c=c, t0=t0, tn=tn: e.tensor_copy(
                xnT[c][:, t0:t0 + tn], ptr_ps[:, 128 * c:128 * c + tn]),
                reads=[ptr_ps], writes=[xnT[c]])
        for c in range(8):
            p.act(lambda e, c=c, t0=t0, tn=tn: e.activation(
                hnT[c][:, t0:t0 + tn], xnT[c][:, t0:t0 + tn], AF.Identity,
                scale=s_lnwb[:, c:c + 1], bias=s_lnwb[:, 8 + c:9 + c]),
                reads=[xnT[c], s_lnwb], writes=[hnT[c]])

    nslices = [(0, 512), (512, 512), (1024, 512), (1536, 512), (2048, 3)]
    ip_ps = p.psum("ip_ps", [128, 512], F32)
    ev = p.sbuf("ev", [128, 512], F32)
    convacc = p.sbuf("convacc", [128, T_CORE], F32)
    xbc_raw = p.sbuf("xbc_raw", [128, TH], F32)
    dt_t = p.sbuf("dt_t", [32, TH], F32)
    da_t = p.sbuf("da_t", [32, TH], F32)
    aneg = p.sbuf("aneg", [32, 1], F32)

    p.act(lambda e: e.activation(aneg[:], s_alog[:], AF.Exp), reads=[s_alog], writes=[aneg])
    p.dve(lambda e: e.tensor_scalar(aneg[:], aneg[:], -1.0, None, op0=ALU.mult),
          reads=[aneg], writes=[aneg])

    n_dchunks = 34
    for dchunk in range(n_dchunks):
        m0 = dchunk * 128
        mn = 128 if dchunk < 33 else 32
        is_z = dchunk < 16
        is_xbc = 16 <= dchunk < 33
        is_dt = dchunk == 33
        for (n0, nn) in nslices:
            def mmfn(e, dchunk=dchunk, m0=m0, mn=mn, n0=n0, nn=nn):
                insts = []
                for c in range(8):
                    insts.append(e.matmul(
                        ip_ps[0:mn, 0:nn],
                        wbfT[c][:, m0:m0 + mn],
                        hnT[c][:, n0:n0 + nn],
                        start=(c == 0), stop=(c == 7)))
                return insts[-1]
            p.pe(mmfn, reads=wbfT + hnT, writes=[ip_ps])
            if is_z:
                p.act(lambda e, mn=mn, nn=nn: e.activation(ev[0:mn, 0:nn],
                                                           ip_ps[0:mn, 0:nn], AF.Copy),
                      reads=[ip_ps], writes=[ev])
                p.dma((zT_out, zT_out[m0:m0 + mn, n0:n0 + nn]), (ev, ev[0:mn, 0:nn]),
                      queue="qact")
            elif is_xbc:
                p.act(lambda e, mn=mn, n0=n0, nn=nn: e.activation(
                    xbc_raw[0:mn, n0:n0 + nn], ip_ps[0:mn, 0:nn], AF.Copy),
                    reads=[ip_ps], writes=[xbc_raw])
            else:
                p.act(lambda e, mn=mn, n0=n0, nn=nn: e.activation(
                    dt_t[0:mn, n0:n0 + nn], ip_ps[0:mn, 0:nn], AF.Copy),
                    reads=[ip_ps], writes=[dt_t])
        if is_xbc:
            j = dchunk - 16
            p.dve(lambda e: e.tensor_tensor(xbc_raw[:, 0:3], xbc_raw[:, 0:3],
                                            s_mask3[:, :], op=ALU.mult),
                  reads=[xbc_raw, s_mask3], writes=[xbc_raw])
            p.dve(lambda e, j=j: e.tensor_scalar(
                convacc[:], xbc_raw[:, 0:T_CORE], s_convw[:, 4 * j:4 * j + 1],
                s_convb[:, j:j + 1], op0=ALU.mult, op1=ALU.add),
                reads=[xbc_raw, s_convw, s_convb], writes=[convacc])
            for k in range(1, 4):
                p.dve(lambda e, j=j, k=k: e.scalar_tensor_tensor(
                    convacc[:], xbc_raw[:, k:k + T_CORE], s_convw[:, 4 * j + k:4 * j + k + 1],
                    convacc[:], op0=ALU.mult, op1=ALU.add),
                    reads=[xbc_raw, s_convw, convacc], writes=[convacc])
            p.act(lambda e: e.activation(convacc[:], convacc[:], AF.Silu),
                  reads=[convacc], writes=[convacc])
            p.dma((xbcT_out, xbcT_out[128 * j:128 * (j + 1), :]), (convacc, convacc[:]),
                  queue="qact")

    p.dve(lambda e: e.tensor_scalar(dt_t[:], dt_t[:], s_dtb[:], None, op0=ALU.add),
          reads=[dt_t, s_dtb], writes=[dt_t])
    sp_abs = da_t
    p.act(lambda e: e.activation(sp_abs[:], dt_t[:], AF.Abs), reads=[dt_t], writes=[sp_abs])
    p.act(lambda e: e.activation(sp_abs[:], sp_abs[:], AF.Exp, scale=-1.0),
          reads=[sp_abs], writes=[sp_abs])
    p.act(lambda e: e.activation(sp_abs[:], sp_abs[:], AF.Ln, bias=1.0),
          reads=[sp_abs], writes=[sp_abs])
    p.act(lambda e: e.activation(dt_t[:], dt_t[:], AF.Relu), reads=[dt_t], writes=[dt_t])
    p.dve(lambda e: e.tensor_tensor(dt_t[:], dt_t[:], sp_abs[:], op=ALU.add),
          reads=[dt_t, sp_abs], writes=[dt_t])
    p.dma((dt_out, dt_out), (dt_t, dt_t[:]), queue="qsp")
    p.dve(lambda e: e.tensor_scalar(da_t[:, 0:TH], dt_t[:], aneg[:], None, op0=ALU.mult),
          reads=[dt_t, aneg, sp_abs], writes=[da_t])
    p.act(lambda e: e.activation(da_t[:], da_t[:], AF.Exp), reads=[da_t], writes=[da_t])
    p.dma((da_out, da_out), (da_t, da_t[:]), queue="qsp")

    load_qkv_weights()
    qkv_ps = p.psum("qkv_ps", [128, 512], F32)
    evb = p.sbuf("evb", [128, 512], BF16)
    qn = [(0, 512), (512, 512), (1024, 512), (1536, 512)]
    for (woff, out_ap, bcol0, scale) in (
            (0, qT_out, 0, 1.0 / 32.0), (1024, kT_out, 8, 1.0), (2048, vT_out, 16, 1.0)):
        for dchunk in range(8):
            m0 = woff + dchunk * 128
            for (n0, nn) in qn:
                def mmfn(e, m0=m0, n0=n0, nn=nn):
                    for c in range(8):
                        inst = e.matmul(qkv_ps[:, 0:nn],
                                        wbfT[c][:, m0:m0 + 128],
                                        xnT[c][:, 3 + n0:3 + n0 + nn],
                                        start=(c == 0), stop=(c == 7))
                    return inst
                p.pe(mmfn, reads=wbfT + xnT, writes=[qkv_ps])
                p.dve(lambda e, nn=nn, bcol=bcol0 + dchunk: e.tensor_scalar(
                    ev[:, 0:nn], qkv_ps[:, 0:nn], s_bqkv[:, bcol:bcol + 1], None,
                    op0=ALU.add), reads=[qkv_ps, s_bqkv], writes=[ev])
                p.act(lambda e, nn=nn, scale=scale: e.activation(
                    evb[:, 0:nn], ev[:, 0:nn], AF.Copy, scale=scale),
                    reads=[ev], writes=[evb])
                p.dma((out_ap, out_ap[m0 - woff:m0 - woff + 128, n0:n0 + nn]),
                      (evb, evb[:, 0:nn]), queue="qact")

    p.finish()
    return nc


def host_L1_inputs(inputs, core):
    b, q = core // 4, core % 4
    x = np.asarray(inputs["L0_attn_h"])
    t0 = q * T_CORE
    if q == 0:
        halo = np.zeros((3, D), np.float32)
    else:
        halo = x[b, t0 - 3:t0]
    xs = np.concatenate([halo, x[b, t0:t0 + T_CORE]], axis=0)
    mask3 = np.zeros((128, 3), np.float32) if q == 0 else np.ones((128, 3), np.float32)
    d = {
        "x": f32c(xs),
        "mask3": mask3,
        "wipT": f32c(np.asarray(inputs["L0_attn_in_proj_w"]).T),
        "lnwb": np.concatenate([colsplit(inputs["L0_attn_ln_w"]),
                                colsplit(inputs["L0_attn_ln_b"])], axis=1),
        "convw": f32c(np.asarray(inputs["L0_attn_conv_w"]).reshape(17, 128, 4)
                      .transpose(1, 0, 2).reshape(128, 68)),
        "convb": colsplit(inputs["L0_attn_conv_b"]),
        "dtbias": f32c(np.asarray(inputs["L0_attn_dt_bias"]).reshape(-1, 1)),
        "alog": f32c(np.asarray(inputs["L0_attn_A_log"]).reshape(-1, 1)),
        "wq": f32c(inputs["L0_ffn_Wq"]),
        "wk": f32c(inputs["L0_ffn_Wk"]),
        "wv": f32c(inputs["L0_ffn_Wv"]),
        "bqkv": np.concatenate([colsplit(inputs["L0_ffn_bq"]),
                                colsplit(inputs["L0_ffn_bk"]),
                                colsplit(inputs["L0_ffn_bv"])], axis=1),
        "ident": np.eye(128, dtype=np.float32).astype(BF16NP),
    }
    return d


def kernel(**inputs):
    raise NotImplementedError("under construction")



def build_L3():
    nc = bass.Bass("TRN2", target_bir_lowering=False)
    T = 8192
    NCH = T // 128
    NH = 8

    x_in = nc.dram_tensor("xtok", [T, 512], F32, kind="ExternalInput").ap()
    btok_in = nc.dram_tensor("btok", [T, 64], F32, kind="ExternalInput").ap()
    bT_in = nc.dram_tensor("bT", [64, T], F32, kind="ExternalInput").ap()
    cT_in = nc.dram_tensor("cT", [64, T], F32, kind="ExternalInput").ap()
    dt8_in = nc.dram_tensor("dt8", [8, T], F32, kind="ExternalInput").ap()
    dtT_in = nc.dram_tensor("dtT_tiles", [128, 8 * NCH], F32, kind="ExternalInput").ap()
    alog8_in = nc.dram_tensor("alog8", [8, 1], F32, kind="ExternalInput").ap()
    identf_in = nc.dram_tensor("identf", [16, 16], F32, kind="ExternalInput").ap()
    y_out = nc.dram_tensor("y", [T, 512], F32, kind="ExternalOutput").ap()

    p = Prog(nc)
    BTbf = p.sbuf("BTbf", [64, T], BF16)
    CTbf = p.sbuf("CTbf", [64, T], BF16)
    stage = p.sbuf("stage", [64, 4096], F32)
    lc = p.sbuf("lc", [8, T], F32)
    lcc = p.sbuf("lcc", [8, T], F32)
    onesc = p.sbuf("onesc", [8, 2048], F32)
    a8 = p.sbuf("a8", [8, 2048], F32)
    acol = p.sbuf("acol", [8, 1], F32)
    dtT = p.sbuf("dtT", [128, 8 * NCH], F32)
    identf = p.sbuf("s_identf", [16, 16], F32)
    Xf = [p.sbuf(f"Xf{j}", [128, 512], F32) for j in range(2)]
    Xbf = [p.sbuf(f"Xbf{j}", [128, 512], BF16) for j in range(3)]
    bstage = [p.sbuf(f"bstage{j}", [128, 64], F32) for j in range(2)]
    Btok = [p.sbuf(f"Btok{j}", [128, 64], BF16) for j in range(3)]
    lrows = [p.sbuf(f"lrows{j}", [1, 1024], F32) for j in range(2)]
    s16a = p.sbuf("s16a", [8, 128], F32)
    s16b = p.sbuf("s16b", [8, 128], F32)
    TTs = p.sbuf("TTs", [128, 16], F32)
    Ecol = p.sbuf("Ecol", [128, 8], F32)
    Ucol = p.sbuf("Ucol", [128, 8], F32)
    maskneg = p.sbuf("maskneg", [128, 128], F32)
    tst = p.sbuf("tst", [128, 128], F32)
    ones1 = p.sbuf("ones1", [1, 128], F32)
    ones64 = p.sbuf("ones64", [1, 64], F32)
    dmw = p.sbuf("dmw", [128, 128], F32)
    wbf = p.sbuf("wbf", [128, 128], BF16)
    mbf = p.sbuf("mbf", [128, 128], BF16)
    ytmp = p.sbuf("ytmp", [128, 64], F32)
    yout = [p.sbuf(f"yout{j}", [128, 512], F32) for j in range(2)]
    xsc = p.sbuf("xsc", [128, 64], BF16)
    Sf = [p.sbuf(f"Sf{h}", [64, 64], F32) for h in range(NH)]
    Sbf = [p.sbuf(f"Sbf{h}", [64, 64], BF16) for h in range(NH)]
    decflat = p.sbuf("decflat", [1, 512], F32)
    dec8 = p.sbuf("dec8", [8, 64], F32)
    decB = p.sbuf("decB", [64, 512], F32)
    zcol = p.constcol(0.0)

    pP1 = p.psum("pP1", [128, 128], F32)
    pGt = p.psum("pGt", [128, 128], F32)
    pY1 = p.psum("pY1", [128, 64], F32)
    pY2 = p.psum("pY2", [128, 64], F32)
    pS = p.psum("pS", [64, 64], F32)
    pTT = p.psum("pTT", [128, 16], F32)
    pdec = p.psum("pdec", [64, 512], F32)

    p.dma(dtT, (dtT_in, dtT_in), queue="qsp")
    p.dma(acol, (alog8_in, alog8_in), queue="qsp")
    p.dma(identf, (identf_in, identf_in), queue="qsp")
    for half in range(2):
        c0 = half * 4096
        p.dma(stage, (bT_in, bT_in[:, c0:c0 + 4096]), queue="qsp")
        p.act(lambda e, c0=c0: e.activation(BTbf[:, c0:c0 + 4096], stage[:], AF.Copy),
              reads=[stage], writes=[BTbf])
        p.dma(stage, (cT_in, cT_in[:, c0:c0 + 4096]), queue="qsp")
        p.act(lambda e, c0=c0: e.activation(CTbf[:, c0:c0 + 4096], stage[:], AF.Copy),
              reads=[stage], writes=[CTbf])

    p.dve(lambda e: e.memset(onesc[:], 1.0), reads=[], writes=[onesc])
    p.dve(lambda e: e.memset(ones1[:], 1.0), reads=[], writes=[ones1])
    p.dve(lambda e: e.memset(ones64[:], 1.0), reads=[], writes=[ones64])
    for h in range(NH):
        p.dve(lambda e, h=h: e.memset(Sf[h][:], 0.0), reads=[], writes=[Sf[h]])
        p.dve(lambda e, h=h: e.memset(Sbf[h][:], 0.0), reads=[], writes=[Sbf[h]])
    p.pool(lambda e: e.iota(tst[:], pattern=[[1, 128]], base=0, channel_multiplier=-1,
                            allow_small_or_imprecise_dtypes=True),
           reads=[], writes=[tst])
    p.dve(lambda e: e.tensor_single_scalar(maskneg[:], tst[:], 0.0, op=ALU.is_lt),
          reads=[tst], writes=[maskneg])
    p.dve(lambda e: e.tensor_scalar(maskneg[:], maskneg[:], -1e30, None, op0=ALU.mult),
          reads=[maskneg], writes=[maskneg])

    p.act(lambda e: e.activation(acol[:], acol[:], AF.Exp), reads=[acol], writes=[acol])
    p.dve(lambda e: e.tensor_scalar(acol[:], acol[:], -1.0, None, op0=ALU.mult),
          reads=[acol], writes=[acol])

    for s4 in range(4):
        c0 = s4 * 2048
        p.dma(a8, (dt8_in, dt8_in[:, c0:c0 + 2048]), queue="qsp")
        p.dve(lambda e: e.tensor_scalar(a8[:], a8[:], acol[:], None, op0=ALU.mult),
              reads=[a8, acol], writes=[a8])
        init = 0.0 if s4 == 0 else lc[:, c0 - 1:c0]
        p.dve(lambda e, c0=c0, init=init: e.tensor_tensor_scan(
            lc[:, c0:c0 + 2048], onesc[:], a8[:], init, op0=ALU.mult, op1=ALU.add),
            reads=[onesc, a8, lc], writes=[lc])
    for i in range(NCH):
        prevcol = zcol[0:8, 0:1] if i == 0 else lc[:, 128 * i - 1:128 * i]
        p.dve(lambda e, i=i, prevcol=prevcol: e.tensor_scalar(
            lcc[:, 128 * i:128 * (i + 1)], lc[:, 128 * i:128 * (i + 1)], prevcol, None,
            op0=ALU.subtract), reads=[lc, zcol], writes=[lcc])
    p.act(lambda e: e.activation(dec8[:], lcc[:, 127::128], AF.Exp),
          reads=[lcc], writes=[dec8])
    p.dma(decflat, (dec8, dec8[:]), queue="qsp")
    p.pe(lambda e: e.matmul(pdec[:], ones64[:], decflat[:], start=True, stop=True),
         reads=[ones64, decflat], writes=[pdec])
    p.act(lambda e: e.activation(decB[:], pdec[:], AF.Copy), reads=[pdec], writes=[decB])

    for i in range(NCH):
        t0 = 128 * i
        xf = Xf[i % 2]
        xb = Xbf[i % 3]
        bs = bstage[i % 2]
        bt = Btok[i % 3]
        p.dma(xf, (x_in, x_in[t0:t0 + 128, :]), queue="qpool")
        p.act(lambda e, xf=xf, xb=xb: e.activation(xb[:], xf[:], AF.Copy),
              reads=[xf], writes=[xb])
        p.dma(bs, (btok_in, btok_in[t0:t0 + 128, :]), queue="qsp")
        p.act(lambda e, bs=bs, bt=bt: e.activation(bt[:], bs[:], AF.Copy),
              reads=[bs], writes=[bt])
        p.dve(lambda e, t0=t0: e.tensor_copy(s16a[:], lcc[:, t0:t0 + 128]),
              reads=[lcc], writes=[s16a])
        p.dve(lambda e, t0=t0: e.tensor_scalar(
            s16b[:], lcc[:, t0:t0 + 128], lcc[:, t0 + 127:t0 + 128], None,
            op0=ALU.subtract), reads=[lcc], writes=[s16b])
        p.pe(lambda e: e.transpose(pTT[:, 0:8], s16a[:], identf[0:8, 0:8]),
             reads=[s16a, identf], writes=[pTT])
        p.pe(lambda e: e.transpose(pTT[:, 8:16], s16b[:], identf[0:8, 0:8]),
             reads=[s16b, identf], writes=[pTT])
        p.dve(lambda e: e.tensor_copy(TTs[:], pTT[:]), reads=[pTT], writes=[TTs])
        p.act(lambda e: e.activation(Ecol[:], TTs[:, 0:8], AF.Exp),
              reads=[TTs], writes=[Ecol])
        p.act(lambda e: e.activation(Ucol[:], TTs[:, 8:16], AF.Exp, scale=-1.0),
              reads=[TTs], writes=[Ucol])
        p.dve(lambda e, i=i: e.tensor_tensor(Ucol[:], Ucol[:], dtT[:, 8 * i:8 * (i + 1)],
                                             op=ALU.mult),
              reads=[Ucol, dtT], writes=[Ucol])
        lr = lrows[i % 2]
        p.dma((lr, lr[0:1, :]), (lcc, lcc[0:8, t0:t0 + 128]), queue="qact")
        yo = yout[i % 2]
        for h in range(NH):
            p.pe(lambda e, h=h, lr=lr: e.matmul(
                pP1[:], ones1[:], lr[0:1, 128 * h:128 * (h + 1)], start=True, stop=True),
                reads=[ones1, lr], writes=[pP1])
            p.dve(lambda e, h=h: e.scalar_tensor_tensor(
                dmw[:], pP1[:], TTs[:, h:h + 1], maskneg[:],
                op0=ALU.subtract, op1=ALU.add),
                reads=[pP1, TTs, maskneg], writes=[dmw])
            p.act(lambda e: e.activation(dmw[:], dmw[:], AF.Exp), reads=[dmw], writes=[dmw])
            p.dve(lambda e, i=i, h=h: e.tensor_scalar(
                wbf[:], dmw[:], dtT[:, 8 * i + h:8 * i + h + 1], None, op0=ALU.mult),
                reads=[dmw, dtT], writes=[wbf])
            p.pe(lambda e, t0=t0: e.matmul(pGt[:], BTbf[0:64, t0:t0 + 128],
                                           CTbf[0:64, t0:t0 + 128], start=True, stop=True),
                 reads=[BTbf, CTbf], writes=[pGt])
            p.dve(lambda e: e.tensor_tensor(mbf[:], pGt[:], wbf[:], op=ALU.mult),
                  reads=[pGt, wbf], writes=[mbf])
            p.pe(lambda e, h=h, xb=xb: e.matmul(pY1[:], mbf[:], xb[:, 64 * h:64 * (h + 1)],
                                                start=True, stop=True),
                 reads=[mbf, xb], writes=[pY1])
            p.pe(lambda e, t0=t0, h=h: e.matmul(pY2[:], CTbf[0:64, t0:t0 + 128], Sbf[h][:],
                                                start=True, stop=True),
                 reads=[CTbf, Sbf[h]], writes=[pY2])
            p.act(lambda e, h=h: e.activation(ytmp[:], pY2[:], AF.Copy,
                                              scale=Ecol[:, h:h + 1]),
                  reads=[pY2, Ecol], writes=[ytmp])
            p.dve(lambda e, h=h, yo=yo: e.tensor_tensor(
                yo[:, 64 * h:64 * (h + 1)], pY1[:], ytmp[:], op=ALU.add),
                reads=[pY1, ytmp], writes=[yo])
            p.act(lambda e, h=h, xb=xb: e.activation(
                xsc[:], xb[:, 64 * h:64 * (h + 1)], AF.Copy, scale=Ucol[:, h:h + 1]),
                reads=[xb, Ucol], writes=[xsc])
            p.pe(lambda e, bt=bt: e.matmul(pS[:], bt[:], xsc[:], start=True, stop=True),
                 reads=[bt, xsc], writes=[pS])
            p.dve(lambda e, h=h, i=i: e.tensor_scalar(
                Sf[h][:], Sf[h][:], decB[:, 64 * h + i:64 * h + i + 1], None,
                op0=ALU.mult),
                reads=[Sf[h], decB], writes=[Sf[h]])
            p.dve(lambda e, h=h: e.tensor_tensor(Sf[h][:], Sf[h][:], pS[:], op=ALU.add),
                  reads=[Sf[h], pS], writes=[Sf[h]])
            p.act(lambda e, h=h: e.activation(Sbf[h][:], Sf[h][:], AF.Copy),
                  reads=[Sf[h]], writes=[Sbf[h]])
        p.dma((y_out, y_out[t0:t0 + 128, :]), (yo, yo[:]), queue="qpool")

    p.finish()
    return nc


def host_L3_inputs(I, L1outs, core):
    b, g = core // 4, core % 4
    xbcT = np.concatenate([np.asarray(L1outs[4 * b + qq]["xbcT"]) for qq in range(4)],
                          axis=1)
    dtf = np.concatenate([np.asarray(L1outs[4 * b + qq]["dt"])[:, 3:] for qq in range(4)],
                         axis=1)
    h0 = 8 * g
    dt8 = dtf[h0:h0 + 8]
    dtT = f32c(dt8.T.reshape(64, 128, 8).transpose(1, 0, 2).reshape(128, 512))
    return {
        "xtok": f32c(xbcT[64 * h0:64 * h0 + 512].T),
        "btok": f32c(xbcT[2048:2112].T),
        "bT": f32c(xbcT[2048:2112]),
        "cT": f32c(xbcT[2112:2176]),
        "dt8": f32c(dt8),
        "dtT_tiles": dtT,
        "alog8": f32c(np.asarray(I["L0_attn_A_log"])[h0:h0 + 8].reshape(8, 1)),
        "identf": np.eye(16, dtype=np.float32),
    }



def build_L4():
    nc = bass.Bass("TRN2", target_bir_lowering=False)
    yT_in = nc.dram_tensor("yT", [2048, T_CORE], F32, kind="ExternalInput").ap()
    xT_in = nc.dram_tensor("xT", [2048, T_CORE], F32, kind="ExternalInput").ap()
    zT_in = nc.dram_tensor("zT2", [2048, T_CORE], F32, kind="ExternalInput").ap()
    xnT_in = nc.dram_tensor("xnT", [D, T_CORE], F32, kind="ExternalInput").ap()
    wopT_in = nc.dram_tensor("wopT", [2048, D], F32, kind="ExternalInput").ap()
    dv_in = nc.dram_tensor("dv64", [128, 16], F32, kind="ExternalInput").ap()
    rw_in = nc.dram_tensor("rmsw", [128, 16], F32, kind="ExternalInput").ap()
    sT_out = nc.dram_tensor("sT", [D, T_CORE], F32, kind="ExternalOutput").ap()

    p = Prog(nc)
    dv = p.sbuf("dv", [128, 16], F32)
    rw = p.sbuf("rw", [128, 16], F32)
    wop = [p.sbuf(f"wop{c}", [128, D], BF16) for c in range(16)]
    wstage = p.sbuf("wstage", [128, D], F32)
    xnT = [p.sbuf(f"xnTs{c}", [128, T_CORE], F32) for c in range(8)]
    yg = [p.sbuf(f"yg{c}", [128, 512], F32) for c in range(16)]
    ygn = [p.sbuf(f"ygn{c}", [128, 512], BF16) for c in range(16)]
    ytile = [p.sbuf(f"ytile{j}", [128, 512], F32) for j in range(2)]
    xtile = [p.sbuf(f"xtile{j}", [128, 512], F32) for j in range(2)]
    ztile = [p.sbuf(f"ztile{j}", [128, 512], F32) for j in range(2)]
    sz = p.sbuf("sz", [128, 512], F32)
    sq = p.sbuf("sq", [128, 512], F32)
    rstd = p.sbuf("rstd", [1, 512], F32)
    rb = p.sbuf("rb", [128, 512], F32)
    onescol = p.sbuf("onescol", [128, 1], F32)
    ones1 = p.sbuf("ones1", [1, 128], F32)
    sev = p.sbuf("sev", [128, 512], F32)
    eps_c = p.constcol(EPS)
    pss = p.psum("pss", [1, 512], F32)
    prb = p.psum("prb", [128, 512], F32)
    pso = p.psum("pso", [128, 512], F32)

    p.dma(dv, (dv_in, dv_in), queue="qsp")
    p.dma(rw, (rw_in, rw_in), queue="qsp")
    p.dve(lambda e: e.memset(onescol[:], 1.0), reads=[], writes=[onescol])
    p.dve(lambda e: e.memset(ones1[:], 1.0), reads=[], writes=[ones1])
    for c in range(16):
        p.dma(wstage, (wopT_in, wopT_in[128 * c:128 * (c + 1), :]), queue="qpool")
        p.act(lambda e, c=c: e.activation(wop[c][:], wstage[:], AF.Copy),
              reads=[wstage], writes=[wop[c]])
    for c in range(8):
        p.dma(xnT[c], (xnT_in, xnT_in[128 * c:128 * (c + 1), :]), queue="qsp")

    for ts4 in range(4):
        n0 = ts4 * 512
        for c in range(16):
            yt = ytile[c % 2]
            xt = xtile[c % 2]
            zt = ztile[c % 2]
            p.dma(yt, (yT_in, yT_in[128 * c:128 * (c + 1), n0:n0 + 512]), queue="qsp")
            p.dma(xt, (xT_in, xT_in[128 * c:128 * (c + 1), n0:n0 + 512]), queue="qpool")
            p.dma(zt, (zT_in, zT_in[128 * c:128 * (c + 1), n0:n0 + 512]), queue="qact")
            p.dve(lambda e, c=c, yt=yt, xt=xt: e.scalar_tensor_tensor(
                yg[c][:], xt[:], dv[:, c:c + 1], yt[:], op0=ALU.mult, op1=ALU.add),
                reads=[xt, dv, yt], writes=[yg[c]])
            p.act(lambda e, zt=zt: e.activation(sz[:], zt[:], AF.Silu),
                  reads=[zt], writes=[sz])
            p.dve(lambda e, c=c: e.tensor_tensor(yg[c][:], yg[c][:], sz[:], op=ALU.mult),
                  reads=[yg[c], sz], writes=[yg[c]])
            p.act(lambda e, c=c: e.activation(sq[:], yg[c][:], AF.Square),
                  reads=[yg[c]], writes=[sq])
            p.pe(lambda e, c=c: e.matmul(pss[:], onescol[:], sq[:],
                                         start=(c == 0), stop=(c == 15)),
                 reads=[onescol, sq], writes=[pss])
        p.act(lambda e: e.activation(rstd[:], pss[:], AF.Ln, scale=1.0 / 2048.0,
                                     bias=eps_c[0:1, 0:1]),
              reads=[pss, eps_c], writes=[rstd])
        p.act(lambda e: e.activation(rstd[:], rstd[:], AF.Exp, scale=-0.5),
              reads=[rstd], writes=[rstd])
        p.pe(lambda e: e.matmul(prb[:], ones1[:], rstd[:], start=True, stop=True),
             reads=[ones1, rstd], writes=[prb])
        p.act(lambda e: e.activation(rb[:], prb[:], AF.Copy), reads=[prb], writes=[rb])
        for c in range(16):
            p.dve(lambda e, c=c: e.tensor_scalar(yg[c][:], yg[c][:], rw[:, c:c + 1],
                                                 None, op0=ALU.mult),
                  reads=[yg[c], rw], writes=[yg[c]])
            p.dve(lambda e, c=c: e.tensor_tensor(ygn[c][:], yg[c][:], rb[:], op=ALU.mult),
                  reads=[yg[c], rb], writes=[ygn[c]])
        for m in range(8):
            def mmfn(e, m=m):
                for c in range(16):
                    inst = e.matmul(pso[:], wop[c][:, 128 * m:128 * (m + 1)], ygn[c][:],
                                    start=(c == 0), stop=(c == 15))
                return inst
            p.pe(mmfn, reads=wop + ygn, writes=[pso])
            p.dve(lambda e, m=m, n0=n0: e.tensor_tensor(
                sev[:], pso[:], xnT[m][:, n0:n0 + 512], op=ALU.add),
                reads=[pso, xnT[m]], writes=[sev])
            p.dma((sT_out, sT_out[128 * m:128 * (m + 1), n0:n0 + 512]), (sev, sev[:]),
                  queue="qact")

    p.finish()
    return nc


def host_L4_inputs(I, L1outs, L3outs, core):
    b, q = core // 4, core % 4
    t0 = q * T_CORE
    yfull = np.concatenate([np.asarray(L3outs[4 * b + g]["y"]) for g in range(4)],
                           axis=1)
    xbcT = np.asarray(L1outs[core]["xbcT"])
    return {
        "yT": f32c(yfull[t0:t0 + T_CORE].T),
        "xT": f32c(xbcT[0:2048]),
        "zT2": f32c(np.asarray(L1outs[core]["zT"])[:, 3:]),
        "xnT": f32c(np.asarray(L1outs[core]["xn"]).T),
        "wopT": f32c(np.asarray(I["L0_attn_out_proj_w"]).T),
        "dv64": colsplit(np.repeat(np.asarray(I["L0_attn_Dvec"]), 64)),
        "rmsw": colsplit(I["L0_attn_rms_w"]),
    }



def build_L6():
    nc = bass.Bass("TRN2", target_bir_lowering=False)
    T = 8192
    qT_in = nc.dram_tensor("qT", [D, T_CORE], BF16, kind="ExternalInput").ap()
    kT_in = nc.dram_tensor("kTfull", [D, T], BF16, kind="ExternalInput").ap()
    v_in = nc.dram_tensor("vfull", [T, D], BF16, kind="ExternalInput").ap()
    xnT_in = nc.dram_tensor("xnT", [D, T_CORE], F32, kind="ExternalInput").ap()
    wo_in = nc.dram_tensor("wo", [D, D], F32, kind="ExternalInput").ap()
    bo_in = nc.dram_tensor("bo", [128, 8], F32, kind="ExternalInput").ap()
    wg_in = nc.dram_tensor("wg", [D, 8], F32, kind="ExternalInput").ap()
    bg_in = nc.dram_tensor("bg", [8, 1], F32, kind="ExternalInput").ap()
    identb_in = nc.dram_tensor("identb", [128, 128], BF16, kind="ExternalInput").ap()
    identf_in = nc.dram_tensor("identf", [16, 16], F32, kind="ExternalInput").ap()
    xmf_out = nc.dram_tensor("xmidTf", [D, T_CORE], F32, kind="ExternalOutput").ap()
    xmb_out = nc.dram_tensor("xmidTb", [D, T_CORE], BF16, kind="ExternalOutput").ap()
    g_out = nc.dram_tensor("gates", [T_CORE, 8], F32, kind="ExternalOutput").ap()

    p = Prog(nc)
    qT = [p.sbuf(f"qT{c}", [128, T_CORE], BF16) for c in range(8)]
    acc = [p.sbuf(f"acc{t}", [128, D], F32) for t in range(16)]
    mcol = p.sbuf("mcol", [128, 16], F32)
    lcol = p.sbuf("lcol", [128, 16], F32)
    KT = [[p.sbuf(f"KT{j}_{c}", [128, 512], BF16) for c in range(8)] for j in range(2)]
    Vt = [[p.sbuf(f"Vt{j}_{c}", [128, D], BF16) for c in range(4)] for j in range(2)]
    rmax = p.sbuf("rmax", [128, 1], F32)
    mn = p.sbuf("mn", [128, 1], F32)
    negmn = p.sbuf("negmn", [128, 1], F32)
    alpha = p.sbuf("alpha", [128, 1], F32)
    rowsum = p.sbuf("rowsum", [128, 1], F32)
    Pbf = p.sbuf("Pbf", [128, 512], BF16)
    PTs = p.sbuf("PTs", [128, 512], BF16)
    identb = p.sbuf("identb_s", [128, 128], BF16)
    identf = p.sbuf("identf_s", [16, 16], F32)
    linv = p.sbuf("linv", [128, 1], F32)
    ob = p.sbuf("ob", [128, D], BF16)
    outT = [p.sbuf(f"outT{c}", [128, T_CORE], BF16) for c in range(8)]
    wo = [p.sbuf(f"wo{c}", [128, D], BF16) for c in range(8)]
    wstage = p.sbuf("wstage", [128, D], F32)
    bo = p.sbuf("bo_s", [128, 8], F32)
    wg = p.sbuf("wg_s", [128, 8 * 8], F32)
    wgb = p.sbuf("wgb", [128, 8 * 8], BF16)
    bg = p.sbuf("bg_s", [8, 1], F32)
    xnTs = [p.sbuf(f"xnTl{j}", [128, 512], F32) for j in range(2)]
    xmf = p.sbuf("xmf", [128, 512], F32)
    xmb = [p.sbuf(f"xmb{c}", [128, 512], BF16) for c in range(8)]
    gsb = p.sbuf("gsb", [8, 512], F32)
    gtok = p.sbuf("gtok", [128, 8], F32)
    gmax = p.sbuf("gmax", [128, 1], F32)
    gsum = p.sbuf("gsum", [128, 1], F32)
    pS = p.psum("pS", [128, 512], F32)
    pPT = p.psum("pPT", [128, 512], BF16)
    pO = [p.psum(f"pO{j}", [128, 512], F32) for j in range(2)]
    pWo = p.psum("pWo", [128, 512], F32)
    pT2 = p.psum("pT2", [128, D], BF16)
    pg = p.psum("pg", [8, 512], F32)
    pgt = p.psum("pgt", [128, 8], F32)

    p.dma(identb, (identb_in, identb_in), queue="qsp")
    p.dma(identf, (identf_in, identf_in), queue="qsp")
    p.dma(bo, (bo_in, bo_in), queue="qsp")
    p.dma(bg, (bg_in, bg_in), queue="qsp")
    for c in range(8):
        p.dma(qT[c], (qT_in, qT_in[128 * c:128 * (c + 1), :]), queue="qsp")
        p.dma((wg, wg[:, 8 * c:8 * (c + 1)]), (wg_in, wg_in[128 * c:128 * (c + 1), :]),
              queue="qsp")
        p.dma(wstage, (wo_in, wo_in[128 * c:128 * (c + 1), :]), queue="qpool")
        p.act(lambda e, c=c: e.activation(wo[c][:], wstage[:], AF.Copy),
              reads=[wstage], writes=[wo[c]])
    p.act(lambda e: e.activation(wgb[:], wg[:], AF.Copy), reads=[wg], writes=[wgb])
    p.dve(lambda e: e.memset(mcol[:], -1e30), reads=[], writes=[mcol])
    p.dve(lambda e: e.memset(lcol[:], 0.0), reads=[], writes=[lcol])
    for t in range(16):
        p.dve(lambda e, t=t: e.memset(acc[t][:], 0.0), reads=[], writes=[acc[t]])

    NKC = 16
    for kc in range(NKC):
        j = kc % 2
        k0 = kc * 512
        for c in range(8):
            p.dma(KT[j][c], (kT_in, kT_in[128 * c:128 * (c + 1), k0:k0 + 512]),
                  queue="qsp")
        for c in range(4):
            p.dma(Vt[j][c], (v_in, v_in[k0 + 128 * c:k0 + 128 * (c + 1), :]),
                  queue="qpool")
        for qt in range(16):
            def mmS(e, j=j, qt=qt):
                for c in range(8):
                    inst = e.matmul(pS[:], qT[c][:, 128 * qt:128 * (qt + 1)],
                                    KT[j][c][:], start=(c == 0), stop=(c == 7))
                return inst
            p.pe(mmS, reads=qT + KT[j], writes=[pS])
            p.dve(lambda e: e.reduce_max(rmax[:], pS[:], axis=AX.X),
                  reads=[pS], writes=[rmax])
            p.dve(lambda e, qt=qt: e.tensor_tensor(mn[:], mcol[:, qt:qt + 1], rmax[:],
                                                   op=ALU.max),
                  reads=[mcol, rmax], writes=[mn])
            p.dve(lambda e: e.tensor_scalar(negmn[:], mn[:], -1.0, None, op0=ALU.mult),
                  reads=[mn], writes=[negmn])
            p.act(lambda e, qt=qt: e.activation(alpha[:], mcol[:, qt:qt + 1], AF.Exp,
                                                bias=negmn[:]),
                  reads=[mcol, negmn], writes=[alpha])
            p.dve(lambda e, qt=qt: e.tensor_copy(mcol[:, qt:qt + 1], mn[:]),
                  reads=[mn], writes=[mcol])
            p.act(lambda e: e.activation(Pbf[:], pS[:], AF.Exp, bias=negmn[:],
                                         accum_out=rowsum[:]),
                  reads=[pS, negmn], writes=[Pbf, rowsum])
            p.dve(lambda e, qt=qt: e.scalar_tensor_tensor(
                lcol[:, qt:qt + 1], lcol[:, qt:qt + 1], alpha[:], rowsum[:],
                op0=ALU.mult, op1=ALU.add),
                reads=[lcol, alpha, rowsum], writes=[lcol])
            for c4 in range(4):
                p.pe(lambda e, c4=c4: e.transpose(
                    pPT[:, 128 * c4:128 * (c4 + 1)], Pbf[:, 128 * c4:128 * (c4 + 1)],
                    identb[:]),
                    reads=[Pbf, identb], writes=[pPT])
            p.dve(lambda e: e.tensor_copy(PTs[:], pPT[:]), reads=[pPT], writes=[PTs])
            p.act(lambda e, qt=qt: e.activation(acc[qt][:], acc[qt][:], AF.Copy,
                                                scale=alpha[:]),
                  reads=[acc[qt], alpha], writes=[acc[qt]])
            for half in range(2):
                def mmPV(e, j=j, half=half):
                    for c4 in range(4):
                        inst = e.matmul(pO[half][:],
                                        PTs[:, 128 * c4:128 * (c4 + 1)],
                                        Vt[j][c4][:, 512 * half:512 * (half + 1)],
                                        start=(c4 == 0), stop=(c4 == 3))
                    return inst
                p.pe(mmPV, reads=[PTs] + Vt[j], writes=[pO[half]])
                p.dve(lambda e, qt=qt, half=half: e.tensor_tensor(
                    acc[qt][:, 512 * half:512 * (half + 1)],
                    acc[qt][:, 512 * half:512 * (half + 1)], pO[half][:], op=ALU.add),
                    reads=[acc[qt], pO[half]], writes=[acc[qt]])

    for qt in range(16):
        p.dve(lambda e, qt=qt: e.reciprocal(linv[:], lcol[:, qt:qt + 1]),
              reads=[lcol], writes=[linv])
        p.dve(lambda e, qt=qt: e.tensor_scalar(acc[qt][:], acc[qt][:], linv[:], None,
                                               op0=ALU.mult),
              reads=[acc[qt], linv], writes=[acc[qt]])
        p.act(lambda e, qt=qt: e.activation(ob[:], acc[qt][:], AF.Copy),
              reads=[acc[qt]], writes=[ob])
        for c in range(8):
            p.pe(lambda e, c=c: e.transpose(pT2[:, 128 * c:128 * (c + 1)],
                                            ob[:, 128 * c:128 * (c + 1)], identb[:]),
                 reads=[ob, identb], writes=[pT2])
        for c in range(8):
            p.dve(lambda e, c=c, qt=qt: e.tensor_copy(
                outT[c][:, 128 * qt:128 * (qt + 1)], pT2[:, 128 * c:128 * (c + 1)]),
                reads=[pT2], writes=[outT[c]])

    for ns in range(4):
        n0 = ns * 512
        xl = xnTs[ns % 2]
        for m in range(8):
            p.dma(xl, (xnT_in, xnT_in[128 * m:128 * (m + 1), n0:n0 + 512]), queue="qact")
            def mmWo(e, m=m, n0=n0):
                for c in range(8):
                    inst = e.matmul(pWo[:], wo[c][:, 128 * m:128 * (m + 1)],
                                    outT[c][:, n0:n0 + 512], start=(c == 0), stop=(c == 7))
                return inst
            p.pe(mmWo, reads=wo + outT, writes=[pWo])
            p.dve(lambda e, m=m: e.tensor_scalar(xmf[:], pWo[:], bo[:, m:m + 1], None,
                                                 op0=ALU.add),
                  reads=[pWo, bo], writes=[xmf])
            p.dve(lambda e, xl=xl: e.tensor_tensor(xmf[:], xmf[:], xl[:], op=ALU.add),
                  reads=[xmf, xl], writes=[xmf])
            p.dma((xmf_out, xmf_out[128 * m:128 * (m + 1), n0:n0 + 512]), (xmf, xmf[:]),
                  queue="qsp")
            p.act(lambda e, m=m: e.activation(xmb[m][:], xmf[:], AF.Copy),
                  reads=[xmf], writes=[xmb[m]])
            p.dma((xmb_out, xmb_out[128 * m:128 * (m + 1), n0:n0 + 512]),
                  (xmb[m], xmb[m][:]), queue="qact")
        def mmG(e):
            for c in range(8):
                inst = e.matmul(pg[:], wgb[:, 8 * c:8 * (c + 1)], xmb[c][:],
                                start=(c == 0), stop=(c == 7))
            return inst
        p.pe(mmG, reads=[wgb] + xmb, writes=[pg])
        p.act(lambda e: e.activation(gsb[:], pg[:], AF.Identity, bias=bg[:]),
              reads=[pg, bg], writes=[gsb])
        for sub in range(4):
            p.pe(lambda e, sub=sub: e.transpose(pgt[:], gsb[0:8, 128 * sub:128 * (sub + 1)],
                                                identf[0:8, 0:8]),
                 reads=[gsb, identf], writes=[pgt])
            p.dve(lambda e: e.tensor_copy(gtok[:], pgt[:]), reads=[pgt], writes=[gtok])
            p.dve(lambda e: e.reduce_max(gmax[:], gtok[:], axis=AX.X),
                  reads=[gtok], writes=[gmax])
            p.dve(lambda e: e.tensor_scalar(gmax[:], gmax[:], -1.0, None, op0=ALU.mult),
                  reads=[gmax], writes=[gmax])
            p.act(lambda e: e.activation(gtok[:], gtok[:], AF.Exp, bias=gmax[:],
                                         accum_out=gsum[:]),
                  reads=[gtok, gmax], writes=[gtok, gsum])
            p.dve(lambda e: e.reciprocal(gsum[:], gsum[:]), reads=[gsum], writes=[gsum])
            p.dve(lambda e: e.tensor_scalar(gtok[:], gtok[:], gsum[:], None, op0=ALU.mult),
                  reads=[gtok, gsum], writes=[gtok])
            p.dma((g_out, g_out[n0 + 128 * sub:n0 + 128 * (sub + 1), :]),
                  (gtok, gtok[:]), queue="qsp")

    p.finish()
    return nc


def host_L6_inputs(I, L1outs, core):
    b, q = core // 4, core % 4
    kT = np.concatenate([np.asarray(L1outs[4 * b + qq]["kT"]) for qq in range(4)], axis=1)
    vT = np.concatenate([np.asarray(L1outs[4 * b + qq]["vT"]) for qq in range(4)], axis=1)
    return {
        "qT": np.asarray(L1outs[core]["qT"]),
        "kTfull": np.ascontiguousarray(kT),
        "vfull": np.ascontiguousarray(vT.T),
        "xnT": f32c(np.asarray(L1outs[core]["xn"]).T),
        "wo": f32c(I["L0_ffn_Wo"]),
        "bo": colsplit(I["L0_ffn_bo"]),
        "wg": f32c(I["L0_ffn_Wg"]),
        "bg": f32c(np.asarray(I["L0_ffn_bg"]).reshape(8, 1)),
        "identb": np.eye(128, dtype=np.float32).astype(BF16NP),
        "identf": np.eye(16, dtype=np.float32),
    }



def build_L7():
    nc = bass.Bass("TRN2", target_bir_lowering=False)
    xmb_in = nc.dram_tensor("xmidTb", [D, T_CORE], BF16, kind="ExternalInput").ap()
    xmf_in = nc.dram_tensor("xmidTf", [D, T_CORE], F32, kind="ExternalInput").ap()
    gT_in = nc.dram_tensor("gatesT", [8, T_CORE], F32, kind="ExternalInput").ap()
    w1_in = nc.dram_tensor("w1", [8, D, 4096], F32, kind="ExternalInput").ap()
    w2_in = nc.dram_tensor("w2", [8, 4096, D], F32, kind="ExternalInput").ap()
    b1_in = nc.dram_tensor("b1c", [128, 32 * 8], F32, kind="ExternalInput").ap()
    b2_in = nc.dram_tensor("b2r", [8, D], F32, kind="ExternalInput").ap()
    sT_in = nc.dram_tensor("sT", [D, T_CORE], F32, kind="ExternalInput").ap()
    xT_in = nc.dram_tensor("xTorig", [D, T_CORE], F32, kind="ExternalInput").ap()
    x1_out = nc.dram_tensor("x1T", [D, T_CORE], F32, kind="ExternalOutput").ap()
    yacc = nc.dram_tensor("yacc", [D, T_CORE], F32).ap()

    p = Prog(nc)
    xmb = [p.sbuf(f"xmb{c}", [128, T_CORE], BF16) for c in range(8)]
    w1 = [p.sbuf(f"w1_{c}", [128, 4096], BF16) for c in range(8)]
    w2 = [p.sbuf(f"w2_{d}", [128, D], BF16) for d in range(32)]
    hm = [p.sbuf(f"hm{d}", [128, 512], BF16) for d in range(32)]
    stage = p.sbuf("stage", [128, 512], F32)
    b1c = p.sbuf("s_b1c", [128, 32 * 8], F32)
    b2b = p.sbuf("b2b", [8, D], BF16)
    grow = p.sbuf("grow", [1, 512], F32)
    gb = p.sbuf("gb", [128, 512], F32)
    ones1 = p.sbuf("ones1", [1, 128], F32)
    ev = p.sbuf("ev", [128, 512], F32)
    ysl = p.sbuf("ysl", [128, 512], F32)
    pH = [p.psum(f"pH{j}", [128, 512], F32) for j in range(2)]
    pE = [p.psum(f"pE{j}", [128, 512], F32) for j in range(2)]
    pG = p.psum("pG", [128, 512], F32)
    pB = p.psum("pB", [128, 512], F32)

    p.dma(b1c, (b1_in, b1_in), queue="qsp")
    for hh in range(2):
        p.dma((stage, stage[0:8, 0:512]), (b2_in, b2_in[:, 512 * hh:512 * (hh + 1)]),
              queue="qsp")
        p.act(lambda e, hh=hh: e.activation(b2b[:, 512 * hh:512 * (hh + 1)],
                                            stage[0:8, 0:512], AF.Copy),
              reads=[stage], writes=[b2b])
    p.dve(lambda e: e.memset(ones1[:], 1.0), reads=[], writes=[ones1])
    p.dve(lambda e: e.memset(ev[:], 0.0), reads=[], writes=[ev])
    for c in range(8):
        p.dma(xmb[c], (xmb_in, xmb_in[128 * c:128 * (c + 1), :]), queue="qsp")
        for ns in range(4):
            p.dma((yacc, yacc[128 * c:128 * (c + 1), 512 * ns:512 * (ns + 1)]),
                  (ev, ev[:]), queue="qpool")

    for ei in range(8):
        for c in range(8):
            for qtr in range(8):
                q0 = qtr * 512
                p.dma(stage, (w1_in, w1_in[ei, 128 * c:128 * (c + 1), q0:q0 + 512]),
                      queue="qact")
                p.act(lambda e, c=c, q0=q0: e.activation(
                    w1[c][:, q0:q0 + 512], stage[:], AF.Copy),
                    reads=[stage], writes=[w1[c]])
        for dd in range(32):
            for hh in range(2):
                p.dma(stage, (w2_in, w2_in[ei, 128 * dd:128 * (dd + 1),
                                           512 * hh:512 * (hh + 1)]), queue="qact")
                p.act(lambda e, dd=dd, hh=hh: e.activation(
                    w2[dd][:, 512 * hh:512 * (hh + 1)], stage[:], AF.Copy),
                    reads=[stage], writes=[w2[dd]])
        for grp in range(4):
            n0 = grp * 512
            p.dma(grow, (gT_in, gT_in[ei:ei + 1, n0:n0 + 512]), queue="qsp")
            p.pe(lambda e: e.matmul(pG[:], ones1[:], grow[0:1, :],
                                    start=True, stop=True),
                 reads=[ones1, grow], writes=[pG])
            p.act(lambda e: e.activation(gb[:], pG[:], AF.Copy), reads=[pG], writes=[gb])
            for dd in range(32):
                ph = pH[dd % 2]
                def mm1(e, dd=dd, n0=n0, ph=ph):
                    for c in range(8):
                        inst = e.matmul(ph[:], w1[c][:, 128 * dd:128 * (dd + 1)],
                                        xmb[c][:, n0:n0 + 512], start=(c == 0),
                                        stop=(c == 7))
                    return inst
                p.pe(mm1, reads=w1 + xmb, writes=[ph])
                p.act(lambda e, dd=dd, ei=ei, ph=ph: e.activation(
                    hm[dd][:], ph[:], AF.Relu, bias=b1c[:, 32 * ei + dd:32 * ei + dd + 1]),
                    reads=[ph, b1c], writes=[hm[dd]])
            for c in range(8):
                pe_ = pE[c % 2]
                def mm2(e, c=c, pe_=pe_):
                    for dd in range(32):
                        inst = e.matmul(pe_[:], w2[dd][:, 128 * c:128 * (c + 1)],
                                        hm[dd][:], start=(dd == 0), stop=(dd == 31))
                    return inst
                p.pe(mm2, reads=w2 + hm, writes=[pe_])
                p.dve(lambda e, pe_=pe_: e.tensor_tensor(ev[:], pe_[:], gb[:], op=ALU.mult),
                      reads=[pe_, gb], writes=[ev])
                p.dma((yacc, yacc[128 * c:128 * (c + 1), n0:n0 + 512]), (ev, ev[:]),
                      queue="qpool", accum=True)

    xmfs = gb
    grow2 = ev
    for hh in range(4):
        p.dma((stage, stage[0:8, 0:512]), (gT_in, gT_in[:, 512 * hh:512 * (hh + 1)]),
              queue="qsp")
        p.act(lambda e, hh=hh: e.activation(
            w2[hh // 2][0:8, 512 * (hh % 2):512 * (hh % 2 + 1)],
            stage[0:8, 0:512], AF.Copy),
            reads=[stage], writes=[w2[hh // 2]])
    for m in range(8):
        for ns in range(4):
            n0 = ns * 512
            p.dma(ysl, (yacc, yacc[128 * m:128 * (m + 1), n0:n0 + 512]), queue="qsp")
            p.dma(xmfs, (xmf_in, xmf_in[128 * m:128 * (m + 1), n0:n0 + 512]), queue="qact")
            p.dma((stage, stage[:, 0:512]),
                  (sT_in, sT_in[128 * m:128 * (m + 1), n0:n0 + 512]), queue="qpool")
            p.dma((grow2, grow2[:, 0:512]),
                  (xT_in, xT_in[128 * m:128 * (m + 1), n0:n0 + 512]), queue="qsp")
            p.pe(lambda e, m=m, ns=ns: e.matmul(
                pB[:], b2b[0:8, 128 * m:128 * (m + 1)],
                w2[ns // 2][0:8, 512 * (ns % 2):512 * (ns % 2 + 1)],
                start=True, stop=True),
                reads=[b2b, w2[ns // 2]], writes=[pB])
            p.dve(lambda e: e.tensor_tensor(ysl[:], ysl[:], pB[:], op=ALU.add),
                  reads=[ysl, pB], writes=[ysl])
            p.dve(lambda e: e.tensor_tensor(ysl[:], ysl[:], xmfs[:], op=ALU.add),
                  reads=[ysl, xmfs], writes=[ysl])
            p.act(lambda e: e.activation(ysl[:], ysl[:], AF.Silu), reads=[ysl], writes=[ysl])
            p.dve(lambda e: e.tensor_tensor(ysl[:], ysl[:], stage[:, 0:512], op=ALU.mult),
                  reads=[ysl, stage], writes=[ysl])
            p.dve(lambda e: e.tensor_tensor(ysl[:], ysl[:], grow2[:, 0:512], op=ALU.add),
                  reads=[ysl, grow2], writes=[ysl])
            p.dma((x1_out, x1_out[128 * m:128 * (m + 1), n0:n0 + 512]), (ysl, ysl[:]),
                  queue="qact")

    p.finish()
    return nc


def host_L7_inputs(I, L1outs, L4outs, L6outs, core):
    b, q = core // 4, core % 4
    t0 = q * T_CORE
    x = np.asarray(I["L0_attn_h"])
    b1 = np.asarray(I["L0_ffn_b1"])
    b1c = np.concatenate([colsplit(b1[e]) for e in range(8)], axis=1)
    return {
        "xmidTb": np.asarray(L6outs[core]["xmidTb"]),
        "xmidTf": f32c(L6outs[core]["xmidTf"]),
        "gatesT": f32c(np.asarray(L6outs[core]["gates"]).T),
        "w1": f32c(I["L0_ffn_W1"]),
        "w2": f32c(I["L0_ffn_W2"]),
        "b1c": b1c,
        "b2r": f32c(I["L0_ffn_b2"]),
        "sT": f32c(L4outs[core]["sT"]),
        "xTorig": f32c(x[b, t0:t0 + T_CORE].T),
    }



def build_L8():
    nc = bass.Bass("TRN2", target_bir_lowering=False)
    x1_in = nc.dram_tensor("x1", [T_CORE, D], F32, kind="ExternalInput").ap()
    wqkvT_in = nc.dram_tensor("wqkvT", [D, 3 * D], F32, kind="ExternalInput").ap()
    dp_in = nc.dram_tensor("decayp", [1, 16], F32, kind="ExternalInput").ap()
    identb_in = nc.dram_tensor("identb", [128, 128], BF16, kind="ExternalInput").ap()
    xn1_out = nc.dram_tensor("xn1", [T_CORE, D], F32, kind="ExternalOutput").ap()
    qphi_out = nc.dram_tensor("qphi", [T_CORE, D], F32, kind="ExternalOutput").ap()
    v1_out = nc.dram_tensor("v1", [T_CORE, D], F32, kind="ExternalOutput").ap()
    osc_out = nc.dram_tensor("oscan", [T_CORE, 2048], F32, kind="ExternalOutput").ap()

    p = Prog(nc)
    wq = [p.sbuf(f"wq{c}", [128, 3 * D], BF16) for c in range(8)]
    wstage = p.sbuf("wstage", [128, 1024], F32)
    xt = p.sbuf("xt", [128, D], F32)
    stats = p.sbuf("stats", [128, 4], F32)
    xn1 = p.sbuf("xn1t", [128, D], F32)
    xnb = p.sbuf("xnb", [128, D], BF16)
    xt8 = p.sbuf("xt8", [128, D], BF16)
    rsc = p.sbuf("rsc", [128, 512], F32)
    msc = p.sbuf("msc", [128, 512], F32)
    qph = p.sbuf("qph", [128, D], F32)
    kph = p.sbuf("kph", [128, D], F32)
    vt = p.sbuf("vt", [128, D], F32)
    kv = p.sbuf("kv", [128, D], F32)
    osc = p.sbuf("osc", [128, 2048], F32)
    MT = [p.sbuf(f"MT{h}", [128, 128], F32) for h in range(16)]
    dpow = p.sbuf("dpow", [1, 2048], F32)
    iorow = p.sbuf("iorow", [1, 128], F32)
    tst = p.sbuf("tst", [128, 128], F32)
    maskneg = p.sbuf("maskneg", [128, 128], F32)
    ld = p.sbuf("ld", [1, 16], F32)
    ldB = p.sbuf("ldB", [128, 16], F32)
    ones1 = p.sbuf("ones1", [1, 128], F32)
    e127 = p.sbuf("e127", [128, 1], F32)
    Skv = [p.sbuf(f"Skv{h}", [1, 64], F32) for h in range(16)]
    Sks = [p.sbuf(f"Sks{h}", [1, 64], F32) for h in range(16)]
    eps_c = p.constcol(EPS)
    identb = p.sbuf("identb_s", [128, 128], BF16)

    pQ = [p.psum(f"pQ{j}", [128, 512], F32) for j in range(2)]
    pT = p.psum("pT", [128, D], BF16)
    pKV = p.psum("pKV", [128, 64], F32)
    pKS = p.psum("pKS", [128, 64], F32)
    pX = p.psum("pX", [1, 64], F32)
    pB2 = p.psum("pB2", [128, 16], F32)

    p.dma(identb, (identb_in, identb_in), queue="qsp")
    p.dma(ld, (dp_in, dp_in), queue="qsp")
    for c in range(8):
        for hh in range(3):
            p.dma(wstage, (wqkvT_in, wqkvT_in[128 * c:128 * (c + 1),
                                              1024 * hh:1024 * (hh + 1)]), queue="qpool")
            p.act(lambda e, c=c, hh=hh: e.activation(
                wq[c][:, 1024 * hh:1024 * (hh + 1)], wstage[:], AF.Copy),
                reads=[wstage], writes=[wq[c]])

    p.pool(lambda e: e.iota(tst[:], pattern=[[1, 128]], base=0, channel_multiplier=-1,
                            allow_small_or_imprecise_dtypes=True),
           reads=[], writes=[tst])
    p.pool(lambda e: e.iota(iorow[:], pattern=[[1, 128]], base=1, channel_multiplier=0,
                            allow_small_or_imprecise_dtypes=True),
           reads=[], writes=[iorow])
    p.pool(lambda e: e.iota(e127[:], pattern=[[0, 1]], base=-127, channel_multiplier=1,
                            allow_small_or_imprecise_dtypes=True),
           reads=[], writes=[e127])
    p.dve(lambda e: e.tensor_single_scalar(e127[:], e127[:], 0.0, op=ALU.is_equal),
          reads=[e127], writes=[e127])
    p.dve(lambda e: e.memset(ones1[:], 1.0), reads=[], writes=[ones1])
    p.dve(lambda e: e.tensor_single_scalar(maskneg[:], tst[:], 0.0, op=ALU.is_lt),
          reads=[tst], writes=[maskneg])
    p.dve(lambda e: e.tensor_scalar(maskneg[:], maskneg[:], -1e30, None, op0=ALU.mult),
          reads=[maskneg], writes=[maskneg])
    p.act(lambda e: e.activation(ld[:], ld[:], AF.Sigmoid), reads=[ld], writes=[ld])
    p.dve(lambda e: e.tensor_single_scalar(ld[:], ld[:], 1e-8, op=ALU.max),
          reads=[ld], writes=[ld])
    p.act(lambda e: e.activation(ld[:], ld[:], AF.Ln), reads=[ld], writes=[ld])
    p.pe(lambda e: e.matmul(pB2[:], ones1[:], ld[:], start=True, stop=True),
         reads=[ones1, ld], writes=[pB2])
    p.act(lambda e: e.activation(ldB[:], pB2[:], AF.Copy), reads=[pB2], writes=[ldB])
    for h in range(16):
        p.dve(lambda e, h=h: e.tensor_scalar(MT[h][:], tst[:], ldB[:, h:h + 1], None,
                                             op0=ALU.mult),
              reads=[tst, ldB], writes=[MT[h]])
        p.dve(lambda e, h=h: e.tensor_tensor(MT[h][:], MT[h][:], maskneg[:], op=ALU.add),
              reads=[MT[h], maskneg], writes=[MT[h]])
        p.act(lambda e, h=h: e.activation(MT[h][:], MT[h][:], AF.Exp),
              reads=[MT[h]], writes=[MT[h]])
        p.act(lambda e, h=h: e.activation(dpow[0:1, 128 * h:128 * (h + 1)], iorow[:],
                                          AF.Exp, scale=ld[0:1, h:h + 1]),
              reads=[iorow, ld], writes=[dpow])
        p.dve(lambda e, h=h: e.memset(Skv[h][:], 0.0), reads=[], writes=[Skv[h]])
        p.dve(lambda e, h=h: e.memset(Sks[h][:], 0.0), reads=[], writes=[Sks[h]])

    def phi_evac(psrc, dst, n0):
        p.act(lambda e: e.activation(rsc[:], psrc[:], AF.Relu), reads=[psrc], writes=[rsc])
        p.dve(lambda e: e.tensor_tensor(msc[:], psrc[:], rsc[:], op=ALU.subtract),
              reads=[psrc, rsc], writes=[msc])
        p.act(lambda e: e.activation(msc[:], msc[:], AF.Exp), reads=[msc], writes=[msc])
        p.dve(lambda e, n0=n0: e.tensor_tensor(dst[:, n0:n0 + 512], rsc[:], msc[:],
                                               op=ALU.add),
              reads=[rsc, msc], writes=[dst])

    for ti in range(16):
        t0 = 128 * ti
        p.dma(xt, (x1_in, x1_in[t0:t0 + 128, :]), queue="qsp")
        p.dve(lambda e: e.reduce_sum(stats[:, 0:1], xt[:], axis=AX.X),
              reads=[xt], writes=[stats])
        p.dve(lambda e: e.tensor_scalar(stats[:, 0:1], stats[:, 0:1], 1.0 / D, -1.0,
                                        op0=ALU.mult, op1=ALU.mult),
              reads=[stats], writes=[stats])
        p.act(lambda e: e.activation(xn1[:], xt[:], AF.Square, bias=stats[:, 0:1],
                                     accum_out=stats[:, 1:2]),
              reads=[xt, stats], writes=[xn1, stats])
        p.act(lambda e: e.activation(stats[:, 2:3], stats[:, 1:2], AF.Ln, scale=1.0 / D,
                                     bias=eps_c[:, 0:1]),
              reads=[stats, eps_c], writes=[stats])
        p.act(lambda e: e.activation(stats[:, 2:3], stats[:, 2:3], AF.Exp, scale=-0.5),
              reads=[stats], writes=[stats])
        p.dve(lambda e: e.tensor_tensor(stats[:, 3:4], stats[:, 0:1], stats[:, 2:3],
                                        op=ALU.mult),
              reads=[stats], writes=[stats])
        p.act(lambda e: e.activation(xn1[:], xt[:], AF.Identity, scale=stats[:, 2:3],
                                     bias=stats[:, 3:4]),
              reads=[xt, stats], writes=[xn1])
        p.dma((xn1_out, xn1_out[t0:t0 + 128, :]), (xn1, xn1[:]), queue="qact")
        p.act(lambda e: e.activation(xnb[:], xn1[:], AF.Copy), reads=[xn1], writes=[xnb])
        for c in range(8):
            p.pe(lambda e, c=c: e.transpose(pT[:, 128 * c:128 * (c + 1)],
                                            xnb[:, 128 * c:128 * (c + 1)], identb[:]),
                 reads=[xnb, identb], writes=[pT])
        p.dve(lambda e: e.tensor_copy(xt8[:], pT[:]), reads=[pT], writes=[xt8])
        for nsl in range(6):
            ps = pQ[nsl % 2]
            def mmq(e, nsl=nsl, ps=ps):
                for c in range(8):
                    inst = e.matmul(ps[:], xt8[:, 128 * c:128 * (c + 1)],
                                    wq[c][:, 512 * nsl:512 * (nsl + 1)],
                                    start=(c == 0), stop=(c == 7))
                return inst
            p.pe(mmq, reads=[xt8] + wq, writes=[ps])
            n0 = 512 * (nsl % 2)
            if nsl < 2:
                phi_evac(ps, qph, n0)
            elif nsl < 4:
                phi_evac(ps, kph, n0)
            else:
                p.act(lambda e, ps=ps, n0=n0: e.activation(vt[:, n0:n0 + 512], ps[:],
                                                           AF.Copy),
                      reads=[ps], writes=[vt])
        p.dma((qphi_out, qphi_out[t0:t0 + 128, :]), (qph, qph[:]), queue="qsp")
        p.dma((v1_out, v1_out[t0:t0 + 128, :]), (vt, vt[:]), queue="qact")
        p.dve(lambda e: e.tensor_tensor(kv[:], kph[:], vt[:], op=ALU.mult),
              reads=[kph, vt], writes=[kv])
        for h in range(16):
            p.pe(lambda e, h=h: e.matmul(pKV[:], MT[h][:], kv[:, 64 * h:64 * (h + 1)],
                                         start=True, stop=(ti == 0)),
                 reads=[MT[h], kv], writes=[pKV])
            if ti > 0:
                p.pe(lambda e, h=h: e.matmul(pKV[:], dpow[0:1, 128 * h:128 * (h + 1)],
                                             Skv[h][:], start=False, stop=True),
                     reads=[dpow, Skv[h]], writes=[pKV])
            p.act(lambda e, h=h: e.activation(osc[:, 128 * h:128 * h + 64], pKV[:],
                                              AF.Copy),
                  reads=[pKV], writes=[osc])
            p.pe(lambda e, h=h: e.matmul(pKS[:], MT[h][:], kph[:, 64 * h:64 * (h + 1)],
                                         start=True, stop=(ti == 0)),
                 reads=[MT[h], kph], writes=[pKS])
            if ti > 0:
                p.pe(lambda e, h=h: e.matmul(pKS[:], dpow[0:1, 128 * h:128 * (h + 1)],
                                             Sks[h][:], start=False, stop=True),
                     reads=[dpow, Sks[h]], writes=[pKS])
            p.act(lambda e, h=h: e.activation(osc[:, 128 * h + 64:128 * (h + 1)], pKS[:],
                                              AF.Copy),
                  reads=[pKS], writes=[osc])
            p.pe(lambda e, h=h: e.matmul(pX[:], e127[:], osc[:, 128 * h:128 * h + 64],
                                         start=True, stop=True),
                 reads=[e127, osc], writes=[pX])
            p.act(lambda e, h=h: e.activation(Skv[h][:], pX[:], AF.Copy),
                  reads=[pX], writes=[Skv[h]])
            p.pe(lambda e, h=h: e.matmul(pX[:], e127[:], osc[:, 128 * h + 64:128 * (h + 1)],
                                         start=True, stop=True),
                 reads=[e127, osc], writes=[pX])
            p.act(lambda e, h=h: e.activation(Sks[h][:], pX[:], AF.Copy),
                  reads=[pX], writes=[Sks[h]])
        p.dma((osc_out, osc_out[t0:t0 + 128, :]), (osc, osc[:]), queue="qpool")

    p.finish()
    return nc


def host_L8_inputs(I, L7outs, core):
    return {
        "x1": f32c(np.asarray(L7outs[core]["x1T"]).T),
        "wqkvT": f32c(np.asarray(I["L1_attn_Wqkv"]).T),
        "decayp": f32c(np.asarray(I["L1_attn_decay_param"]).reshape(1, 16)),
        "identb": np.eye(128, dtype=np.float32).astype(BF16NP),
    }



def build_L9():
    nc = bass.Bass("TRN2", target_bir_lowering=False)
    qphi_in = nc.dram_tensor("qphi", [T_CORE, D], F32, kind="ExternalInput").ap()
    osc_in = nc.dram_tensor("oscan", [T_CORE, 2048], F32, kind="ExternalInput").ap()
    v1_in = nc.dram_tensor("v1", [T_CORE, D], F32, kind="ExternalInput").ap()
    prevL_in = nc.dram_tensor("prevL", [4, 2048], F32, kind="ExternalInput").ap()
    shifts_in = nc.dram_tensor("shifts", [4, 1], F32, kind="ExternalInput").ap()
    dp_in = nc.dram_tensor("decayp", [1, 16], F32, kind="ExternalInput").ap()
    wgT_in = nc.dram_tensor("wgT", [D, D], F32, kind="ExternalInput").ap()
    bg_in = nc.dram_tensor("bgate", [128, 8], F32, kind="ExternalInput").ap()
    woT_in = nc.dram_tensor("woT", [D, D], F32, kind="ExternalInput").ap()
    identb_in = nc.dram_tensor("identb", [128, 128], BF16, kind="ExternalInput").ap()
    identf_in = nc.dram_tensor("identf128", [128, 128], F32, kind="ExternalInput").ap()
    s1_out = nc.dram_tensor("s1T", [D, T_CORE], F32, kind="ExternalOutput").ap()
    outa_d = nc.dram_tensor("outa_d", [T_CORE, D], F32).ap()
    g_d = nc.dram_tensor("g_d", [T_CORE, D], F32).ap()

    p = Prog(nc)
    wg = [p.sbuf(f"wg{c}", [128, D], BF16) for c in range(8)]
    wo = [p.sbuf(f"wo{c}", [128, D], BF16) for c in range(8)]
    wstage = p.sbuf("wstage", [128, 1024], F32)
    outaT = [p.sbuf(f"outaT{c}", [128, T_CORE], BF16) for c in range(8)]
    o2T = [p.sbuf(f"o2T{c}", [128, T_CORE], BF16) for c in range(8)]
    identb = p.sbuf("identb_s", [128, 128], BF16)
    identf = p.sbuf("identf_s", [128, 128], F32)
    ld = p.sbuf("ld", [1, 16], F32)
    ones4 = p.sbuf("ones4", [1, 4], F32)
    ld4 = p.sbuf("ld4", [4, 16], F32)
    w4 = p.sbuf("w4", [4, 16], F32)
    shf = p.sbuf("shf", [4, 1], F32)
    prevL = p.sbuf("prevL_s", [4, 2048], F32)
    SB = p.sbuf("SB", [1, 2048], F32)
    SBb = p.sbuf("SBb", [1, 2048], BF16)
    dpow = p.sbuf("dpow", [1, 2048], F32)
    dpowb = p.sbuf("dpowb", [1, 2048], BF16)
    iorow = p.sbuf("iorow", [1, 128], F32)
    qph = p.sbuf("qph", [128, D], F32)
    osc = p.sbuf("osc", [128, 2048], F32)
    vt = p.sbuf("vt", [128, D], F32)
    dtm = p.sbuf("dtm", [128, 64], F32)
    den = p.sbuf("den", [128, 16], F32)
    outa = p.sbuf("outa", [128, D], F32)
    outab = p.sbuf("outab", [128, D], BF16)
    gsl = p.sbuf("gsl", [128, 512], F32)
    gtk = p.sbuf("gtk", [128, D], F32)
    bg = p.sbuf("bg_s", [128, 8], F32)
    bl = p.sbuf("bl", [128, D], F32)
    blb = p.sbuf("blb", [128, D], BF16)
    sev = p.sbuf("sev", [128, 512], F32)
    pC = p.psum("pC", [128, 128], F32)
    pSin = p.psum("pSin", [1, 512], F32)
    pG = p.psum("pG", [128, 512], F32)
    pGT = p.psum("pGT", [128, 512], F32)
    pT2 = p.psum("pT2", [128, D], BF16)
    pWo = p.psum("pWo", [128, 512], F32)

    p.dma(identb, (identb_in, identb_in), queue="qsp")
    p.dma(identf, (identf_in, identf_in), queue="qsp")
    p.dma(ld, (dp_in, dp_in), queue="qsp")
    p.dma(shf, (shifts_in, shifts_in), queue="qsp")
    p.dma(prevL, (prevL_in, prevL_in), queue="qsp")
    p.dma(bg, (bg_in, bg_in), queue="qsp")
    for c in range(8):
        p.dma(wstage, (wgT_in, wgT_in[128 * c:128 * (c + 1), :]), queue="qpool")
        p.act(lambda e, c=c: e.activation(wg[c][:], wstage[:], AF.Copy),
              reads=[wstage], writes=[wg[c]])
        p.dma(wstage, (woT_in, woT_in[128 * c:128 * (c + 1), :]), queue="qpool")
        p.act(lambda e, c=c: e.activation(wo[c][:], wstage[:], AF.Copy),
              reads=[wstage], writes=[wo[c]])
    p.dve(lambda e: e.memset(ones4[:], 1.0), reads=[], writes=[ones4])
    p.pool(lambda e: e.iota(iorow[:], pattern=[[1, 128]], base=1, channel_multiplier=0,
                            allow_small_or_imprecise_dtypes=True),
           reads=[], writes=[iorow])
    p.act(lambda e: e.activation(ld[:], ld[:], AF.Sigmoid), reads=[ld], writes=[ld])
    p.dve(lambda e: e.tensor_single_scalar(ld[:], ld[:], 1e-8, op=ALU.max),
          reads=[ld], writes=[ld])
    p.act(lambda e: e.activation(ld[:], ld[:], AF.Ln), reads=[ld], writes=[ld])
    p.pe(lambda e: e.matmul(pC[0:4, 0:16], ones4[:], ld[:], start=True, stop=True),
         reads=[ones4, ld], writes=[pC])
    p.act(lambda e: e.activation(ld4[:], pC[0:4, 0:16], AF.Copy), reads=[pC], writes=[ld4])
    p.dve(lambda e: e.tensor_scalar(w4[:], ld4[:], shf[:], None, op0=ALU.mult),
          reads=[ld4, shf], writes=[w4])
    p.act(lambda e: e.activation(w4[:], w4[:], AF.Exp), reads=[w4], writes=[w4])
    for h in range(16):
        p.act(lambda e, h=h: e.activation(dpow[0:1, 128 * h:128 * (h + 1)], iorow[:],
                                          AF.Exp, scale=ld[0:1, h:h + 1]),
              reads=[iorow, ld], writes=[dpow])
    p.act(lambda e: e.activation(dpowb[:], dpow[:], AF.Copy), reads=[dpow], writes=[dpowb])
    for half in range(4):
        for hh in range(4):
            h = 4 * half + hh
            p.pe(lambda e, h=h, hh=hh: e.matmul(
                pSin[0:1, 128 * hh:128 * (hh + 1)], w4[:, h:h + 1],
                prevL[:, 128 * h:128 * (h + 1)], start=True, stop=True),
                reads=[w4, prevL], writes=[pSin])
        p.act(lambda e, half=half: e.activation(SB[0:1, 512 * half:512 * (half + 1)],
                                                pSin[0:1, :], AF.Copy),
              reads=[pSin], writes=[SB])
    p.act(lambda e: e.activation(SBb[:], SB[:], AF.Copy), reads=[SB], writes=[SBb])

    for ti in range(16):
        t0 = 128 * ti
        p.dma(osc, (osc_in, osc_in[t0:t0 + 128, :]), queue="qsp")
        p.dma(qph, (qphi_in, qphi_in[t0:t0 + 128, :]), queue="qact")
        for h in range(16):
            p.pe(lambda e, h=h: e.matmul(pC[:], dpowb[0:1, 128 * h:128 * (h + 1)],
                                         SBb[0:1, 128 * h:128 * (h + 1)],
                                         start=True, stop=True),
                 reads=[dpowb, SBb], writes=[pC])
            p.dve(lambda e, h=h: e.tensor_tensor(osc[:, 128 * h:128 * (h + 1)],
                                                 osc[:, 128 * h:128 * (h + 1)], pC[:],
                                                 op=ALU.add),
                  reads=[osc, pC], writes=[osc])
            p.dve(lambda e, h=h: e.tensor_tensor(
                dtm[:], qph[:, 64 * h:64 * (h + 1)],
                osc[:, 128 * h + 64:128 * (h + 1)], op=ALU.mult),
                reads=[qph, osc], writes=[dtm])
            p.dve(lambda e, h=h: e.reduce_sum(den[:, h:h + 1], dtm[:], axis=AX.X),
                  reads=[dtm], writes=[den])
        p.dve(lambda e: e.tensor_single_scalar(den[:], den[:], 1e-6, op=ALU.max),
              reads=[den], writes=[den])
        p.dve(lambda e: e.reciprocal(den[:], den[:]), reads=[den], writes=[den])
        for h in range(16):
            p.dve(lambda e, h=h: e.tensor_tensor(
                outa[:, 64 * h:64 * (h + 1)], qph[:, 64 * h:64 * (h + 1)],
                osc[:, 128 * h:128 * h + 64], op=ALU.mult),
                reads=[qph, osc], writes=[outa])
            p.dve(lambda e, h=h: e.tensor_scalar(
                outa[:, 64 * h:64 * (h + 1)], outa[:, 64 * h:64 * (h + 1)],
                den[:, h:h + 1], None, op0=ALU.mult),
                reads=[outa, den], writes=[outa])
        p.dma((outa_d, outa_d[t0:t0 + 128, :]), (outa, outa[:]), queue="qpool")
        p.act(lambda e: e.activation(outab[:], outa[:], AF.Copy),
              reads=[outa], writes=[outab])
        for c in range(8):
            p.pe(lambda e, c=c: e.transpose(pT2[:, 128 * c:128 * (c + 1)],
                                            outab[:, 128 * c:128 * (c + 1)], identb[:]),
                 reads=[outab, identb], writes=[pT2])
        for c in range(8):
            p.dve(lambda e, c=c, ti=ti: e.tensor_copy(
                outaT[c][:, 128 * ti:128 * (ti + 1)], pT2[:, 128 * c:128 * (c + 1)]),
                reads=[pT2], writes=[outaT[c]])

    for ns in range(4):
        n0 = ns * 512
        for m in range(8):
            def mmg(e, m=m, n0=n0):
                for c in range(8):
                    inst = e.matmul(pG[:], wg[c][:, 128 * m:128 * (m + 1)],
                                    outaT[c][:, n0:n0 + 512], start=(c == 0), stop=(c == 7))
                return inst
            p.pe(mmg, reads=wg + outaT, writes=[pG])
            p.act(lambda e, m=m: e.activation(gsl[:], pG[:], AF.Sigmoid,
                                              bias=bg[:, m:m + 1]),
                  reads=[pG, bg], writes=[gsl])
            for sub in range(4):
                p.pe(lambda e, sub=sub: e.transpose(
                    pGT[:, 128 * sub:128 * (sub + 1)],
                    gsl[:, 128 * sub:128 * (sub + 1)], identf[:]),
                    reads=[gsl, identf], writes=[pGT])
            for sub in range(4):
                p.dve(lambda e, sub=sub, m=m: e.tensor_copy(
                    gtk[:, 128 * m:128 * (m + 1)], pGT[:, 128 * sub:128 * (sub + 1)]),
                    reads=[pGT], writes=[gtk])
                p.dma((g_d, g_d[n0 + 128 * sub:n0 + 128 * (sub + 1),
                               128 * m:128 * (m + 1)]),
                      (gtk, gtk[:, 128 * m:128 * (m + 1)]), queue="qpool")

    for ti in range(16):
        t0 = 128 * ti
        p.dma(outa, (outa_d, outa_d[t0:t0 + 128, :]), queue="qsp")
        p.dma(vt, (v1_in, v1_in[t0:t0 + 128, :]), queue="qact")
        p.dma(gtk, (g_d, g_d[t0:t0 + 128, :]), queue="qpool")
        p.dve(lambda e: e.tensor_tensor(bl[:], outa[:], vt[:], op=ALU.subtract),
              reads=[outa, vt], writes=[bl])
        p.dve(lambda e: e.tensor_tensor(bl[:], gtk[:], bl[:], op=ALU.mult),
              reads=[gtk, bl], writes=[bl])
        p.dve(lambda e: e.tensor_tensor(bl[:], vt[:], bl[:], op=ALU.add),
              reads=[vt, bl], writes=[bl])
        p.act(lambda e: e.activation(blb[:], bl[:], AF.Copy), reads=[bl], writes=[blb])
        for c in range(8):
            p.pe(lambda e, c=c: e.transpose(pT2[:, 128 * c:128 * (c + 1)],
                                            blb[:, 128 * c:128 * (c + 1)], identb[:]),
                 reads=[blb, identb], writes=[pT2])
        for c in range(8):
            p.dve(lambda e, c=c, ti=ti: e.tensor_copy(
                o2T[c][:, 128 * ti:128 * (ti + 1)], pT2[:, 128 * c:128 * (c + 1)]),
                reads=[pT2], writes=[o2T[c]])
    for ns in range(4):
        n0 = ns * 512
        for m in range(8):
            def mmo(e, m=m, n0=n0):
                for c in range(8):
                    inst = e.matmul(pWo[:], wo[c][:, 128 * m:128 * (m + 1)],
                                    o2T[c][:, n0:n0 + 512], start=(c == 0), stop=(c == 7))
                return inst
            p.pe(mmo, reads=wo + o2T, writes=[pWo])
            p.act(lambda e: e.activation(sev[:], pWo[:], AF.Copy),
                  reads=[pWo], writes=[sev])
            p.dma((s1_out, s1_out[128 * m:128 * (m + 1), n0:n0 + 512]), (sev, sev[:]),
                  queue="qsp")

    p.finish()
    return nc


def host_L9_inputs(I, L8outs, core):
    b, q = core // 4, core % 4
    prevL = np.stack([np.asarray(L8outs[4 * b + p_]["oscan"])[-1] for p_ in range(4)])
    shifts = np.array([[(q - 1 - p_) * 2048.0 if p_ < q else 1e9] for p_ in range(4)],
                      dtype=np.float32)
    return {
        "qphi": f32c(L8outs[core]["qphi"]),
        "oscan": f32c(L8outs[core]["oscan"]),
        "v1": f32c(L8outs[core]["v1"]),
        "prevL": f32c(prevL),
        "shifts": shifts,
        "decayp": f32c(np.asarray(I["L1_attn_decay_param"]).reshape(1, 16)),
        "wgT": f32c(np.asarray(I["L1_attn_Wgate"]).T),
        "bgate": colsplit(I["L1_attn_bgate"]),
        "woT": f32c(np.asarray(I["L1_attn_Wout"]).T),
        "identb": np.eye(128, dtype=np.float32).astype(BF16NP),
        "identf128": np.eye(128, dtype=np.float32),
    }



def build_L10():
    nc = bass.Bass("TRN2", target_bir_lowering=False)
    xnh_in = nc.dram_tensor("xn1Th", [D, T_CORE + 1], F32, kind="ExternalInput").ap()
    wrT_in = nc.dram_tensor("wrT", [D, D], F32, kind="ExternalInput").ap()
    wkT_in = nc.dram_tensor("wkT", [D, D], F32, kind="ExternalInput").ap()
    wvT_in = nc.dram_tensor("wvT", [D, D], F32, kind="ExternalInput").ap()
    mur_in = nc.dram_tensor("murc", [128, 8], F32, kind="ExternalInput").ap()
    muk_in = nc.dram_tensor("mukc", [128, 8], F32, kind="ExternalInput").ap()
    s1_in = nc.dram_tensor("s1T", [D, T_CORE], F32, kind="ExternalInput").ap()
    x1_in = nc.dram_tensor("x1T", [D, T_CORE], F32, kind="ExternalInput").ap()
    out_T = nc.dram_tensor("outT", [D, T_CORE], F32, kind="ExternalOutput").ap()

    p = Prog(nc)
    wr = [p.sbuf(f"wr{c}", [128, D], BF16) for c in range(8)]
    wk = [p.sbuf(f"wk{c}", [128, D], BF16) for c in range(8)]
    wv = [p.sbuf(f"wv{c}", [128, D], BF16) for c in range(8)]
    wstage = p.sbuf("wstage", [128, 1024], F32)
    xnh = [p.sbuf(f"xnh{c}", [128, 1025], F32) for c in range(8)]
    mur = p.sbuf("mur", [128, 8], F32)
    muk = p.sbuf("muk", [128, 8], F32)
    imur = p.sbuf("imur", [128, 8], F32)
    imuk = p.sbuf("imuk", [128, 8], F32)
    mx = [p.sbuf(f"mx{c}", [128, 1024], BF16) for c in range(8)]
    kact = [p.sbuf(f"kact{c}", [128, 1024], BF16) for c in range(8)]
    oT = [p.sbuf(f"oT{c}", [128, 1024], F32) for c in range(8)]
    t1 = p.sbuf("t1", [128, 1024], F32)
    rtmp = p.sbuf("rtmp", [128, 512], F32)
    fsl = p.sbuf("fsl", [128, 512], F32)
    fsl2 = p.sbuf("fsl2", [128, 512], F32)
    pM = [p.psum(f"pM{j}", [128, 512], F32) for j in range(2)]

    p.dma(mur, (mur_in, mur_in), queue="qsp")
    p.dma(muk, (muk_in, muk_in), queue="qsp")
    p.dve(lambda e: e.tensor_scalar(imur[:], mur[:], -1.0, 1.0, op0=ALU.mult, op1=ALU.add),
          reads=[mur], writes=[imur])
    p.dve(lambda e: e.tensor_scalar(imuk[:], muk[:], -1.0, 1.0, op0=ALU.mult, op1=ALU.add),
          reads=[muk], writes=[imuk])
    for (win, dst) in ((wrT_in, wr), (wkT_in, wk), (wvT_in, wv)):
        for c in range(8):
            dt_ = dst[c]
            p.dma(wstage, (win, win[128 * c:128 * (c + 1), :]), queue="qpool")
            p.act(lambda e, dt_=dt_: e.activation(dt_[:], wstage[:], AF.Copy),
                  reads=[wstage], writes=[dt_])

    for half in range(2):
        n0 = half * 1024
        for c in range(8):
            p.dma(xnh[c], (xnh_in, xnh_in[128 * c:128 * (c + 1), n0:n0 + 1025]),
                  queue="qsp")
        for c in range(8):
            p.dve(lambda e, c=c: e.tensor_scalar(t1[:], xnh[c][:, 1:1025],
                                                 muk[:, c:c + 1], None, op0=ALU.mult),
                  reads=[xnh[c], muk], writes=[t1])
            p.dve(lambda e, c=c: e.scalar_tensor_tensor(
                mx[c][:], xnh[c][:, 0:1024], imuk[:, c:c + 1], t1[:],
                op0=ALU.mult, op1=ALU.add),
                reads=[xnh[c], imuk, t1], writes=[mx[c]])
        for m in range(8):
            for ns in range(2):
                ps = pM[ns % 2]
                def mmk(e, m=m, ns=ns, ps=ps):
                    for c in range(8):
                        inst = e.matmul(ps[:], wk[c][:, 128 * m:128 * (m + 1)],
                                        mx[c][:, 512 * ns:512 * (ns + 1)],
                                        start=(c == 0), stop=(c == 7))
                    return inst
                p.pe(mmk, reads=wk + mx, writes=[ps])
                p.act(lambda e, ps=ps: e.activation(rtmp[:], ps[:], AF.Relu),
                      reads=[ps], writes=[rtmp])
                p.act(lambda e, m=m, ns=ns: e.activation(
                    kact[m][:, 512 * ns:512 * (ns + 1)], rtmp[:], AF.Square),
                    reads=[rtmp], writes=[kact[m]])
        for m in range(8):
            for ns in range(2):
                ps = pM[ns % 2]
                def mmv(e, m=m, ns=ns, ps=ps):
                    for c in range(8):
                        inst = e.matmul(ps[:], wv[c][:, 128 * m:128 * (m + 1)],
                                        kact[c][:, 512 * ns:512 * (ns + 1)],
                                        start=(c == 0), stop=(c == 7))
                    return inst
                p.pe(mmv, reads=wv + kact, writes=[ps])
                p.act(lambda e, m=m, ns=ns, ps=ps: e.activation(
                    oT[m][:, 512 * ns:512 * (ns + 1)], ps[:], AF.Copy),
                    reads=[ps], writes=[oT[m]])
        for c in range(8):
            p.dve(lambda e, c=c: e.tensor_scalar(t1[:], xnh[c][:, 1:1025],
                                                 mur[:, c:c + 1], None, op0=ALU.mult),
                  reads=[xnh[c], mur], writes=[t1])
            p.dve(lambda e, c=c: e.scalar_tensor_tensor(
                mx[c][:], xnh[c][:, 0:1024], imur[:, c:c + 1], t1[:],
                op0=ALU.mult, op1=ALU.add),
                reads=[xnh[c], imur, t1], writes=[mx[c]])
        for m in range(8):
            for ns in range(2):
                ps = pM[ns % 2]
                def mmr(e, m=m, ns=ns, ps=ps):
                    for c in range(8):
                        inst = e.matmul(ps[:], wr[c][:, 128 * m:128 * (m + 1)],
                                        mx[c][:, 512 * ns:512 * (ns + 1)],
                                        start=(c == 0), stop=(c == 7))
                    return inst
                p.pe(mmr, reads=wr + mx, writes=[ps])
                p.act(lambda e, ps=ps: e.activation(rtmp[:], ps[:], AF.Sigmoid),
                      reads=[ps], writes=[rtmp])
                nn0 = n0 + 512 * ns
                p.dve(lambda e, m=m, ns=ns: e.tensor_tensor(
                    rtmp[:], rtmp[:], oT[m][:, 512 * ns:512 * (ns + 1)], op=ALU.mult),
                    reads=[rtmp, oT[m]], writes=[rtmp])
                p.act(lambda e: e.activation(rtmp[:], rtmp[:], AF.Silu),
                      reads=[rtmp], writes=[rtmp])
                p.dma(fsl, (s1_in, s1_in[128 * m:128 * (m + 1), nn0:nn0 + 512]),
                      queue="qact")
                p.dma(fsl2, (x1_in, x1_in[128 * m:128 * (m + 1), nn0:nn0 + 512]),
                      queue="qpool")
                p.dve(lambda e: e.tensor_tensor(rtmp[:], rtmp[:], fsl[:], op=ALU.mult),
                      reads=[rtmp, fsl], writes=[rtmp])
                p.dve(lambda e: e.tensor_tensor(rtmp[:], rtmp[:], fsl2[:], op=ALU.add),
                      reads=[rtmp, fsl2], writes=[rtmp])
                p.dma((out_T, out_T[128 * m:128 * (m + 1), nn0:nn0 + 512]),
                      (rtmp, rtmp[:]), queue="qsp")

    p.finish()
    return nc


def host_L10_inputs(I, L7outs, L8outs, L9outs, core):
    b, q = core // 4, core % 4
    xn1 = np.asarray(L8outs[core]["xn1"])
    if q == 0:
        halo = np.zeros((1, D), np.float32)
    else:
        halo = np.asarray(L8outs[core - 1]["xn1"])[-1:]
    xnh = np.concatenate([halo, xn1], axis=0)
    return {
        "xn1Th": f32c(xnh.T),
        "wrT": f32c(np.asarray(I["L1_ffn_W_r"]).T),
        "wkT": f32c(np.asarray(I["L1_ffn_W_k"]).T),
        "wvT": f32c(np.asarray(I["L1_ffn_W_v"]).T),
        "murc": colsplit(I["L1_ffn_mu_r"]),
        "mukc": colsplit(I["L1_ffn_mu_k"]),
        "s1T": f32c(L9outs[core]["s1T"]),
        "x1T": f32c(L7outs[core]["x1T"]),
    }



_NC_CACHE = {}


def _get_nc(name, builder):
    if name not in _NC_CACHE:
        _NC_CACHE[name] = builder()
    return _NC_CACHE[name]


def kernel(**inputs):
    I = inputs
    cores = list(range(NCORES))

    nc1 = _get_nc("L1", build_L1)
    r1 = run_bass_kernel_spmd(nc1, [host_L1_inputs(I, c) for c in cores], core_ids=cores)
    L1outs = r1.results

    nc3 = _get_nc("L3", build_L3)
    r3 = run_bass_kernel_spmd(nc3, [host_L3_inputs(I, L1outs, c) for c in cores],
                              core_ids=cores)
    L3outs = r3.results

    nc4 = _get_nc("L4", build_L4)
    r4 = run_bass_kernel_spmd(nc4, [host_L4_inputs(I, L1outs, L3outs, c) for c in cores],
                              core_ids=cores)
    L4outs = r4.results

    nc6 = _get_nc("L6", build_L6)
    r6 = run_bass_kernel_spmd(nc6, [host_L6_inputs(I, L1outs, c) for c in cores],
                              core_ids=cores)
    L6outs = r6.results

    nc7 = _get_nc("L7", build_L7)
    r7 = run_bass_kernel_spmd(nc7, [host_L7_inputs(I, L1outs, L4outs, L6outs, c)
                                    for c in cores], core_ids=cores)
    L7outs = r7.results

    nc8 = _get_nc("L8", build_L8)
    r8 = run_bass_kernel_spmd(nc8, [host_L8_inputs(I, L7outs, c) for c in cores],
                              core_ids=cores)
    L8outs = r8.results

    nc9 = _get_nc("L9", build_L9)
    r9 = run_bass_kernel_spmd(nc9, [host_L9_inputs(I, L8outs, c) for c in cores],
                              core_ids=cores)
    L9outs = r9.results

    nc10 = _get_nc("L10", build_L10)
    r10 = run_bass_kernel_spmd(nc10, [host_L10_inputs(I, L7outs, L8outs, L9outs, c)
                                      for c in cores], core_ids=cores)
    L10outs = r10.results

    out = np.empty((2, 8192, D), dtype=np.float32)
    for c in cores:
        b, q = c // 4, c % 4
        out[b, q * T_CORE:(q + 1) * T_CORE] = np.asarray(L10outs[c]["outT"]).T
    return out
